# version 11
# baseline (speedup 1.0000x reference)
import numpy as np
import ml_dtypes
import concourse.bass as bass
import concourse.mybir as mybir
from concourse.bass_utils import run_bass_kernel_spmd

BF = ml_dtypes.bfloat16
F32 = mybir.dt.float32
BF16 = mybir.dt.bfloat16
AF = mybir.ActivationFunctionType
ALU = mybir.AluOpType
AX = mybir.AxisListType

B, T, D = 2, 4096, 2048
TC = 1024
NCORES = 8
HALF = 64
SCALE = float(128.0 ** 0.5)
LAMBDA_INIT = 0.8 - 0.6 * float(np.exp(-0.3 * 12))
EPS = 1e-5
MODW = 3072
GROUPS = [[0, 1, 2, 3], [4, 5, 6, 7]]
ROPE0 = 8192
NUMOFF = [(s // 3) * 512 + (s % 3) * 129 for s in range(8)]
TSLOT = [0, 1024, 2048, 3072]


def _rope_tables():
    inv = 1.0 / 10000.0 ** (np.arange(0, HALF, 2)[: HALF // 2].astype(np.float64) / HALF)
    ang = np.outer(np.arange(T), inv)
    return np.cos(ang).astype(np.float32), np.sin(ang).astype(np.float32)


def _host_consts(core):
    tq = core % 4
    cos, sin = _rope_tables()
    sl = slice(tq * TC, (tq + 1) * TC)
    c, s = cos[sl].T, sin[sl].T
    idx = (np.arange(128) % HALF) // 2
    Crow = c[idx]
    Srow = s[idx] * np.where(np.arange(128) % 2 == 0, -1.0, 1.0)[:, None]
    psw = np.zeros((128, 128), np.float32)
    i = np.arange(0, 128, 2)
    psw[i, i + 1] = 1.0
    psw[i + 1, i] = 1.0
    return {
        "ropecq": np.ascontiguousarray((Crow * SCALE).astype(BF)),
        "ropesq": np.ascontiguousarray((Srow * SCALE).astype(BF)),
        "ropeck": np.ascontiguousarray(Crow.astype(BF)),
        "ropesk": np.ascontiguousarray(Srow.astype(BF)),
        "pswap": psw.astype(BF),
        "identb": np.eye(128, dtype=np.float32).astype(BF),
        "identf": np.eye(128, dtype=np.float32),
        "ones128b": np.ones((128, 1), np.float32).astype(BF),
        "ones1f": np.ones((1, 128), np.float32),
        "ones1b": np.ones((1, 128), np.float32).astype(BF),
    }


class _S:
    def __init__(self, handle, name):
        self.h = handle
        self.name = name
        self.n = 0


class _Sched:
    QUEUES = ("pe", "act", "dve", "pool", "syncq")

    def __init__(self):
        self.ops = {q: [] for q in self.QUEUES}
        self.guar = {q: {} for q in self.QUEUES}
        self.trace = []
        self.sdve = None
        self.dve_pend = 0

    def emit(self, q, builder, waits=(), inc=None, inck=1, dvw=None):
        if q == "dve" and self.dve_pend:
            waits = list(waits) + [(self.sdve, self.dve_pend)]
            self.dve_pend = 0
        g = self.guar[q]
        baked = []
        rec_w = []
        for (s, v) in waits:
            rec_w.append((s.name, v))
            if v > 0 and g.get(s.name, 0) < v:
                baked.append((s.h, v))
                g[s.name] = v
        if q == "dve" and dvw is not None and dvw < 64 and inc is None:
            inc = self.sdve
        self.trace.append((q, tuple(rec_w), (inc.name if inc else None), inck,
                           getattr(builder, "__name__", "op")))
        if inc is not None:
            inc.n += inck
        if q == "dve" and dvw is not None and dvw < 64:
            self.dve_pend = inc.n
        def f(eng, baked=baked, builder=builder, ih=(inc.h if inc else None), ik=inck):
            for h, v in baked:
                eng.wait_ge(h, v)
            ins = builder(eng)
            if ih is not None:
                ins.then_inc(ih, ik)
        self.ops[q].append(f)


def build_program(dbg=()):
    nc = bass.Bass("TRN2", target_bir_lowering=False)
    dbg = set(dbg)

    dti = lambda name, shape, dty=F32: nc.dram_tensor(name, shape, dty, kind="ExternalInput").ap()
    x_d = dti("x_sl", [TC, D])
    wq_d = dti("wq", [D, D]); wk_d = dti("wk", [D, D]); wv_d = dti("wv", [D, D]); wo_d = dti("wout", [D, D])
    wfc_d = dti("wfc", [8, D, D]); wpj_d = dti("wproj", [8, D, D])
    wrt_d = dti("wrouter", [D, 8])
    wmod_d = dti("wmod_sl", [D, MODW])
    c2_d = dti("c2", [2, D])
    lam_d = dti("lams", [1, 4 * HALF])
    sub_d = dti("subln", [1, 128])
    cq_d = dti("ropecq", [128, TC], BF16); sq_d = dti("ropesq", [128, TC], BF16)
    ck_d = dti("ropeck", [128, TC], BF16); sk_d = dti("ropesk", [128, TC], BF16)
    psw_d = dti("pswap", [128, 128], BF16)
    idb_d = dti("identb", [128, 128], BF16)
    idf_d = dti("identf", [128, 128], F32)
    o128_d = dti("ones128b", [128, 1], BF16)
    o1f_d = dti("ones1f", [1, 128], F32)
    o1b_d = dti("ones1b", [1, 128], BF16)

    y_d = nc.dram_tensor("y", [TC, D], F32, kind="ExternalOutput").ap()
    DBG_MENU = {
        "h1": ([128, 16384], BF16), "qrot": ([128, 16384], BF16),
        "kcin": ([4097, TC], BF16), "modv": ([128, 96], F32),
        "camod": ([128, 96], F32), "negcb": ([128, 32], F32),
        "lamb": ([128, 1], F32), "attnt": ([128, 16384], BF16),
        "x1": ([TC, D], F32), "h2": ([128, 16384], BF16),
        "wtok": ([128, 64], F32), "acc": ([128, 16384], BF16),
        "nstat": ([1, 128], F32), "ltile": ([128, 256], BF16),
        "negm": ([2, 2048], BF16), "mrow": ([128, 32], F32),
        "numcp": ([128, 1536], F32),
        "lraw": ([1, 256], BF16), "lsmall": ([128, 32], F32),
        "p1small": ([128, 32], F32), "p1xn": ([128, 2048], BF16),
    }
    dbg_d = {n: nc.dram_tensor("dbg_" + n, DBG_MENU[n][0], DBG_MENU[n][1],
                               kind="ExternalOutput").ap() for n in dbg}

    wq_bf = nc.dram_tensor("wq_bf", [D, D], BF16)
    wk_bf = nc.dram_tensor("wk_bf", [D, D], BF16)
    wv_bf = nc.dram_tensor("wv_bf", [D, D], BF16)
    wo_bf = nc.dram_tensor("wo_bf", [D, D], BF16)
    wfcb = nc.dram_tensor("wfcb", [8, D, D], BF16)
    wpjb = nc.dram_tensor("wpjb", [8, D, D], BF16)
    wmod_bf = nc.dram_tensor("wmod_bf", [D, MODW], BF16)
    kv_cin = nc.dram_tensor("kv_cin", [4097, TC], BF16)
    kv_out = nc.dram_tensor("kv_out", [4 * 4097, TC], BF16)
    mod_cin = nc.dram_tensor("mod_cin", [1, MODW], F32)
    mod_out = nc.dram_tensor("mod_out", [4, MODW], F32)
    x1_dram = nc.dram_tensor("x1_dram", [TC, D], F32)

    from contextlib import ExitStack
    stk = ExitStack()
    sb = lambda name, shape, dty: stk.enter_context(nc.sbuf_tensor(name, shape, dty))
    psb = lambda name, shape, dty: stk.enter_context(nc.psum_tensor(name, shape, dty))

    reg_h = sb("reg_h", [128, 16384], BF16)
    reg_q = sb("reg_q", [128, 16384], BF16)
    reg_e = sb("reg_e", [128, 32768], BF16)
    reg_k = sb("reg_k", [128, 16448], BF16)
    strm = sb("strm", [128, 9216], BF16)
    scr1 = sb("scr1", [128, 1024], F32)
    scr2 = sb("scr2", [128, 1024], F32)
    numcp = sb("numcp", [128, 1536], F32)
    xn_s = sb("xn_s", [128, 2048], BF16)
    qraw = sb("qraw", [128, 1024], BF16)
    small = sb("small", [128, 32], F32)
    modv = sb("modv", [128, 96], F32)
    camod = sb("camod", [128, 96], F32)
    negcb = sb("negcb", [128, 32], F32)
    lamb = sb("lamb", [128, 1], F32)
    sublb = sb("sublb", [128, 128], F32)
    wrtb = sb("wrtb", [128, 128], BF16)
    wtok = sb("wtok", [128, 64], F32)
    logit = sb("logit", [128, 32], F32)
    nstat = sb("nstat", [1, 128], F32)
    k2loc = sb("k2loc", [1, 32], F32)
    k2bf = sb("k2bf", [1, 128], BF16)
    modm = sb("modm", [96, 128], F32)
    cid_b = sb("cid_b", [128, 128], BF16)
    cid_f = sb("cid_f", [128, 128], F32)
    cpsw = sb("cpsw", [128, 128], BF16)
    co128 = sb("co128", [128, 1], BF16)
    co1f = sb("co1f", [1, 128], F32)
    co1b = sb("co1b", [1, 128], BF16)
    silct = sb("silct", [128, 32], BF16)
    piece = sb("piece", [128, 256], BF16)
    ones65 = sb("ones65", [65, 128], BF16)
    attbuf = [(scr2[:, 0:128], scr2[:, 128:256], scr2[:, 256:384]),
              (scr2[:, 384:512], scr2[:, 512:640], scr2[:, 640:768])]
    wtb = scr2[:, 0:1024]
    csilb = reg_e[0:2, 4096:6144]
    ltile_t = sb("ltile", [128, 256], BF16)
    ltile = ltile_t[:, :]
    lraw_t = sb("lraw", [1, 256], BF16)
    lraw = lraw_t[:, :]
    sraw_t = sb("sraw", [1, 128], F32)
    sraw = sraw_t[:, :]

    ps_a = psb("ps_a", [128, 1024], F32)
    ps_b = psb("ps_b", [128, 1024], F32)
    ps_c = psb("ps_c", [128, 2048], F32)
    ps_cb = ps_c.bitcast(BF16)
    ps_ab = ps_a.bitcast(BF16)

    sem = lambda name: _S(stk.enter_context(nc.semaphore(name)), name)
    s_ld = sem("s_ld"); s_gd = sem("s_gd")
    s_pe = sem("s_pe"); s_act = sem("s_act"); s_dve = sem("s_dve")
    s_cc = sem("s_cc"); s_out = sem("s_out")

    sch = _Sched()
    sch.sdve = s_dve
    E = sch.emit

    def ld(dst_ap, src_ap, q="syncq", waits=()):
        s = s_ld if q == "syncq" else s_gd
        E(q, lambda e, d=dst_ap, sr=src_ap: e.dma_start(out=d, in_=sr),
          waits=waits, inc=s, inck=16)
        return s.n

    def ldt(dst_ap, src_ap, waits=()):
        E("syncq", lambda e, d=dst_ap, sr=src_ap: e.dma_start(out=d, in_=sr, transpose=True),
          waits=waits, inc=s_ld, inck=16)
        return s_ld.n

    def pemm(out_ap, l_ap, r_ap, start=True, stop=True, waits=(), inc=None):
        E("pe", lambda e, o=out_ap, l=l_ap, r=r_ap, st=start, sp=stop:
          e.matmul(o, l, r, start=st, stop=sp), waits=waits, inc=inc)
        return s_pe.n

    def petr(out_ap, in_ap, ident_ap, waits=(), inc=None):
        E("pe", lambda e, o=out_ap, i=in_ap, d=ident_ap: e.transpose(o, i, d),
          waits=waits, inc=inc)
        return s_pe.n

    def act(out_ap, in_ap, func, bias=None, scale=1.0, accum=None, waits=(), inc=None):
        def f(e, o=out_ap, i=in_ap, fn=func, b=bias, s=scale, a=accum):
            kw = dict(func=fn, scale=s)
            if b is not None:
                kw["bias"] = b
            if a is not None:
                kw["accum_out"] = a
            return e.activation(out=o, in_=i, **kw)
        E("act", f, waits=waits, inc=inc)
        return s_act.n

    def _w(ap):
        try:
            sh = list(ap.shape)
        except Exception:
            return None
        n = 1
        for d in sh[1:]:
            n *= int(d)
        return n

    def dvts(out_ap, in_ap, sc1, sc2, op0, op1=None, waits=(), inc=None):
        def f(e, o=out_ap, i=in_ap, a=sc1, b=sc2, p0=op0, p1=op1):
            if p1 is None:
                return e.tensor_scalar(o, i, a, None, p0)
            return e.tensor_scalar(o, i, a, b, p0, p1)
        E("dve", f, waits=waits, inc=inc, dvw=_w(out_ap))
        return s_dve.n

    def dvtt(out_ap, a_ap, b_ap, op, waits=(), inc=None):
        E("dve", lambda e, o=out_ap, a=a_ap, b=b_ap, p=op: e.tensor_tensor(o, a, b, p),
          waits=waits, inc=inc, dvw=_w(out_ap))
        return s_dve.n

    def dvred(out_ap, in_ap, op, waits=(), inc=None):
        E("dve", lambda e, o=out_ap, i=in_ap, p=op: e.tensor_reduce(o, i, AX.X, p),
          waits=waits, inc=inc, dvw=_w(out_ap))
        return s_dve.n

    def dvcp(out_ap, in_ap, waits=(), inc=None):
        E("dve", lambda e, o=out_ap, i=in_ap: e.tensor_copy(o, i), waits=waits, inc=inc,
          dvw=_w(out_ap))
        return s_dve.n

    def dvrecip(out_ap, in_ap, waits=(), inc=None):
        E("dve", lambda e, o=out_ap, i=in_ap: e.reciprocal(o, i), waits=waits, inc=inc,
          dvw=_w(out_ap))
        return s_dve.n

    def dvmemset(ap, val, waits=(), inc=None):
        E("dve", lambda e, a=ap, v=val: e.memset(a, v), waits=waits, inc=inc,
          dvw=_w(ap))
        return s_dve.n

    ld(cid_b[:, :], idb_d)
    ld(cid_f[:, :], idf_d)
    ld(cpsw[:, :], psw_d)
    ld(co128[:, :], o128_d)
    ld(co1f[:, :], o1f_d)
    ld(co1b[:, :], o1b_d)
    ld(reg_k[:, ROPE0:ROPE0 + 1024], cq_d)
    ld(reg_k[:, ROPE0 + 1024:ROPE0 + 2048], sq_d)
    ld(reg_k[:, ROPE0 + 2048:ROPE0 + 3072], ck_d)
    ld(reg_k[:, ROPE0 + 3072:ROPE0 + 4096], sk_d)
    ld(lraw, lam_d, q="pool")
    ld(wrtb[:, :].rearrange("p (k e) -> p k e", e=8),
       wrt_d.rearrange("(k p) e -> p k e", p=128), q="pool")
    ld(sraw, sub_d)
    ld(csilb, c2_d, q="pool")
    E("pool", lambda e: e.dma_start(out=wmod_bf.ap()[:, :], in_=wmod_d),
      inc=s_gd, inck=16)
    GD_WMOD = s_gd.n
    P0_LD = s_ld.n
    P0_GD = s_gd.n
    dvmemset(small[:, 8:9], EPS)
    ONES33 = dvmemset(ones65[:, :], 1.0, inc=s_dve)

    peL = pemm(ps_c[:, 0:256], co1b[:, :], lraw,
               waits=[(s_ld, P0_LD), (s_gd, P0_GD)], inc=s_pe)
    aL = act(ltile, ps_c[:, 0:256], AF.Copy, waits=[(s_pe, peL)], inc=s_act)
    dvtt(scr1[:, 0:64], ltile[:, 0:64], ltile[:, 64:128], ALU.mult, waits=[(s_act, aL)])
    dvred(small[:, 16:17], scr1[:, 0:64], ALU.add)
    dvtt(scr1[:, 0:64], ltile[:, 128:192], ltile[:, 192:256], ALU.mult)
    dL = dvred(small[:, 17:18], scr1[:, 0:64], ALU.add, inc=s_dve)
    act(small[:, 18:19], small[:, 16:17], AF.Exp, waits=[(s_dve, dL)])
    aL2 = act(small[:, 19:20], small[:, 17:18], AF.Exp, inc=s_act)
    dvtt(small[:, 20:21], small[:, 18:19], small[:, 19:20], ALU.subtract,
         waits=[(s_act, aL2)])
    dvts(lamb[:, 0:1], small[:, 20:21], LAMBDA_INIT, None, ALU.add, inc=s_dve)

    if "lraw" in dbg:
        ld(dbg_d["lraw"], lraw, waits=[(s_gd, P0_GD)])
    if "ltile" in dbg:
        ld(dbg_d["ltile"], ltile, waits=[(s_act, aL)])
    if "lsmall" in dbg:
        LAMD = s_dve.n
        ld(dbg_d["lsmall"], small[:, :], waits=[(s_dve, LAMD)])
    peS = pemm(ps_c[:, 1024:1152], co1f[:, :], sraw, waits=[(s_act, aL)], inc=s_pe)
    aS = act(sublb[:, :], ps_c[:, 1024:1152], AF.Copy, waits=[(s_pe, peS)], inc=s_act)

    aSilb = act(csilb, csilb, AF.Silu,
                waits=[(s_ld, P0_LD), (s_gd, P0_GD)], inc=s_act)
    tr_done = []
    for k in range(16):
        slot = TSLOT[k % 4] + 512
        w = [(s_act, aSilb), (s_pe, peS)]
        if k >= 4:
            w.append((s_act, tr_done[k - 4]))
        pk = petr(ps_cb[:, slot:slot + 2], csilb[0:2, k * 128:(k + 1) * 128], cid_b[0:2, 0:2],
                  waits=w, inc=s_pe)
        tr_done.append(act(silct[:, 2 * k:2 * k + 2], ps_cb[:, slot:slot + 2], AF.Copy,
                           waits=[(s_pe, pk)], inc=s_act))
    aSilt = tr_done[-1]
    mod_slices = [(ps_a, 0), (ps_a, 512), (ps_b, 0), (ps_b, 512), (ps_c, 0), (ps_c, 512)]
    mod_pe = []
    for k in range(16):
        slot = (k % 2) * 3072
        w = [(s_gd, GD_WMOD)]
        if k >= 2:
            w.append((s_pe, mod_pe[k - 2]))
        gk = ld(strm[:, slot:slot + 3072], wmod_bf.ap()[k * 128:(k + 1) * 128, :],
                waits=w)
        for nb in range(6):
            tgt, off = mod_slices[nb]
            pemm(tgt[0:2, off:off + 512], silct[:, 2 * k:2 * k + 2],
                 strm[:, slot + nb * 512:slot + (nb + 1) * 512],
                 start=(k == 0), stop=(k == 15),
                 waits=([(s_ld, gk), (s_act, aSilt)] if nb == 0 else []),
                 inc=(s_pe if nb == 5 else None))
        mod_pe.append(s_pe.n)
    mod_ld = 0
    for nb in range(6):
        tgt, off = mod_slices[nb]
        w_ev = [(s_pe, mod_pe[-1])]
        if nb > 0:
            w_ev.append((s_gd, mod_ld))
        aM = act(scr1[0:1, 0:512], tgt[0:1, off:off + 512], AF.Copy,
                 waits=w_ev, inc=s_act)
        mod_ld = ld(mod_cin.ap()[:, nb * 512:(nb + 1) * 512], scr1[0:1, 0:512],
                    q="pool", waits=[(s_act, aM)])
    E("pool", lambda e: e.collective_compute(
        "AllGather", ALU.bypass, replica_groups=GROUPS,
        ins=[mod_cin.ap().opt()], outs=[mod_out.ap().opt()]),
      waits=[(s_gd, mod_ld)], inc=s_cc)
    CC_MOD = s_cc.n
    mm_ld = ld(modm[:, :], mod_out.ap().rearrange("r (g p) -> (r g) p", p=128),
               q="pool", waits=[(s_cc, CC_MOD)])
    peMV = petr(ps_c[:, 1024:1120], modm[:, :], cid_f[0:96, 0:96],
                waits=[(s_gd, mm_ld), (s_act, aS)], inc=s_pe)
    aMV = act(modv[:, :], ps_c[:, 1024:1120], AF.Copy, waits=[(s_pe, peMV)], inc=s_act)
    dvts(camod[:, 0:16], modv[:, 16:32], 1.0, None, ALU.add, waits=[(s_act, aMV)])
    dvcp(camod[:, 16:32], modv[:, 0:16])
    dvts(camod[:, 32:48], modv[:, 64:80], 1.0, None, ALU.add)
    dvcp(camod[:, 48:64], modv[:, 48:64])
    for (vsrc, vdst) in ((32, 64), (80, 80)):
        dvts(scr1[:, 0:16], modv[:, vsrc:vsrc + 16], -1.0, None, ALU.mult)
        dvtt(scr1[:, 16:32], modv[:, vsrc:vsrc + 16], scr1[:, 0:16], ALU.max)
        dvts(scr1[:, 32:48], scr1[:, 16:32], 0.25, None, ALU.max)
        dvts(scr1[:, 48:64], modv[:, vsrc:vsrc + 16], 0.0, None, ALU.is_ge)
        dvts(scr1[:, 0:16], scr1[:, 48:64], 2.0, 1.0, ALU.mult, ALU.subtract)
        dvtt(camod[:, vdst:vdst + 16], scr1[:, 32:48], scr1[:, 0:16], ALU.mult)
    DVE_MOD = dvmemset(scr1[:, 63:64], 0.0, inc=s_dve)

    reg_kf = reg_k.bitcast(F32)
    h1 = reg_h
    qbuf = reg_q
    JUNK = strm[:, 8192:9216]
    p1_evac = []
    for tt in range(8):
        xst = reg_kf[:, (tt % 2) * 2048:(tt % 2) * 2048 + 2048]
        w = [(s_dve, p1_evac[tt - 2])] if tt >= 2 else []
        xl = ld(xst, x_d[tt * 128:(tt + 1) * 128, :], waits=w)
        P_ = (tt % 2) * 16
        act(reg_e[:, 0:2048], xst, AF.Square, accum=small[:, P_ + 3:P_ + 4],
            waits=[(s_ld, xl)])
        act(reg_e[:, 2048:4096], xst, AF.Copy, accum=small[:, P_ + 2:P_ + 3])
        aST = act(small[:, 10:11], small[:, 8:9], AF.Copy, inc=s_act)
        dvts(small[:, P_ + 0:P_ + 1], small[:, P_ + 2:P_ + 3], 1.0 / D, None, ALU.mult,
             waits=[(s_act, aST)])
        dvts(small[:, P_ + 4:P_ + 5], small[:, P_ + 3:P_ + 4], 1.0 / D, None, ALU.mult)
        dvtt(small[:, P_ + 5:P_ + 6], small[:, P_ + 0:P_ + 1], small[:, P_ + 0:P_ + 1],
             ALU.mult)
        dV = dvtt(small[:, P_ + 6:P_ + 7], small[:, P_ + 4:P_ + 5],
                  small[:, P_ + 5:P_ + 6], ALU.subtract, inc=s_dve)
        act(small[:, P_ + 5:P_ + 6], small[:, P_ + 6:P_ + 7], AF.Ln,
            bias=small[:, 8:9], waits=[(s_dve, dV)])
        aR = act(small[:, P_ + 1:P_ + 2], small[:, P_ + 5:P_ + 6], AF.Exp, scale=-0.5,
                 inc=s_act)
        dXN = dvts(xn_s[:, :], xst, small[:, P_ + 0:P_ + 1], small[:, P_ + 1:P_ + 2],
                   ALU.subtract, ALU.mult, waits=[(s_act, aR)], inc=s_dve)
        evac_hist = []
        for k in range(16):
            slot = TSLOT[2 + (k % 2)]
            w2 = [(s_dve, dXN), (s_act, aMV)]
            if k >= 2:
                w2.append((s_dve, evac_hist[k - 2]))
            pT = petr(ps_cb[:, slot:slot + 128], xn_s[:, k * 128:(k + 1) * 128],
                      cid_b[:, :], waits=w2, inc=s_pe)
            evac_hist.append(
                dvts(h1[:, k * 1024 + tt * 128:k * 1024 + tt * 128 + 128],
                     ps_cb[:, slot:slot + 128],
                     camod[:, k:k + 1], camod[:, 16 + k:16 + k + 1], ALU.mult, ALU.add,
                     waits=[(s_pe, pT), (s_dve, DVE_MOD)], inc=s_dve))
        p1_evac.append(s_dve.n)
        P1_ACT = aST
        if tt == 7 and "h1" in dbg:
            ld(dbg_d["h1"], h1[:, :], waits=[(s_dve, s_dve.n)])
        if tt == 7 and "p1small" in dbg:
            ld(dbg_d["p1small"], small[:, :], waits=[(s_dve, dXN)])
        if tt == 7 and "p1xn" in dbg:
            ld(dbg_d["p1xn"], xn_s[:, :], waits=[(s_dve, dXN)])
    P1_DVE = s_dve.n

    CQ = reg_k[:, ROPE0:ROPE0 + 1024]
    SQ = reg_k[:, ROPE0 + 1024:ROPE0 + 2048]
    CK = reg_k[:, ROPE0 + 2048:ROPE0 + 3072]
    SK = reg_k[:, ROPE0 + 3072:ROPE0 + 4096]
    kst = [reg_k[:, 12288:13312], reg_k[:, 13312:14336]]
    vtm_view = kv_cin.ap()[2048:4096, :].rearrange("(t two) c -> t (two c)", two=2)

    for (src, dst) in ((wq_d, wq_bf), (wk_d, wk_bf), (wv_d, wv_bf), (wo_d, wo_bf)):
        E("pool", lambda e, s=src, d=dst: e.dma_start(out=d.ap()[:, :], in_=s),
          inc=s_gd, inck=16)
    GD_WO = s_gd.n
    GD_W = {"q": GD_WO - 3, "k": GD_WO - 2, "v": GD_WO - 1}

    prev_w_pe = 0
    P2_done = {"act": 0, "dve": 0}
    kv_st = []
    for wi, wname in enumerate(("q", "k", "v")):
        wbf = (wq_bf, wk_bf, wv_bf)[wi]
        wts = [(s_gd, GD_W[wname])] + ([(s_pe, prev_w_pe)] if wi else
                                             [(s_act, P1_ACT), (s_act, aSilt),
                                              (s_pe, mod_pe[-1])])
        for k in range(16):
            ldt(reg_e[:, k * 2048:(k + 1) * 2048], wbf.ap()[:, k * 128:(k + 1) * 128],
                waits=(wts if k == 0 else []))
        trd = s_ld.n
        evac_act = []
        evac_dve = []
        vstores = []
        for m in range(16):
            tgt = ps_a if m % 2 == 0 else ps_b
            w_p = [(s_ld, trd)]
            if m >= 2:
                w_p.append((s_act, evac_act[m - 2]))
            elif wi > 0:
                w_p.append((s_act, P2_done["act"]))
            if wi == 0 and m == 0:
                w_p.append((s_dve, P1_DVE))
            for nh in range(2):
                for k in range(16):
                    pemm(tgt[:, nh * 512:(nh + 1) * 512],
                         reg_e[:, k * 2048 + m * 128:k * 2048 + m * 128 + 128],
                         h1[:, k * 1024 + nh * 512:k * 1024 + nh * 512 + 512],
                         start=(k == 0), stop=(k == 15),
                         waits=(w_p if (nh == 0 and k == 0) else []),
                         inc=(s_pe if (nh == 1 and k == 15) else None))
            mmd = s_pe.n
            w_e = [(s_pe, mmd)]
            if evac_dve:
                w_e.append((s_dve, evac_dve[-1]))
            elif wi > 0:
                w_e.append((s_dve, P2_done["dve"]))
            aQ = act(qraw[:, :], tgt[:, :], AF.Copy, waits=w_e, inc=s_act)
            evac_act.append(aQ)
            if wname in ("q", "k"):
                coff = (m % 2) * 1024
                w_s = [(s_act, aQ)]
                if m >= 2:
                    w_s.append((s_dve, evac_dve[m - 2]))
                elif wi > 0:
                    w_s.append((s_dve, P2_done["dve"]))
                for nh in range(2):
                    pemm(ps_c[:, coff + nh * 512:coff + (nh + 1) * 512],
                         cpsw[:, :], qraw[:, nh * 512:(nh + 1) * 512],
                         waits=(w_s if nh == 0 else []),
                         inc=(s_pe if nh == 1 else None))
                swd = s_pe.n
                Ct, St = (CQ, SQ) if wname == "q" else (CK, SK)
                dvtt(scr1[:, :], qraw[:, :], Ct, ALU.mult, waits=[(s_act, aQ)])
                dvtt(scr2[:, :], ps_c[:, coff:coff + 1024], St, ALU.mult,
                     waits=[(s_pe, swd)])
                dstt = qbuf[:, m * 1024:(m + 1) * 1024] if wname == "q" else kst[m % 2]
                dR = dvtt(dstt, scr1[:, :], scr2[:, :], ALU.add, inc=s_dve)
                if wname == "k":
                    kv_st.append(ld(kv_cin.ap()[m * 128:(m + 1) * 128, :], dstt,
                                    waits=[(s_dve, dR)]))
                evac_dve.append(dR)
            else:
                for tj in range(8):
                    slot = TSLOT[tj % 4]
                    w_t = [(s_act, aQ)]
                    if len(vstores) >= 4:
                        w_t.append((s_act, vstores[-4][0]))
                    else:
                        w_t.append((s_dve, P2_done["dve"]))
                    pT = petr(ps_cb[:, slot:slot + 128],
                              qraw[:, tj * 128:(tj + 1) * 128], cid_b[:, :],
                              waits=w_t, inc=s_pe)
                    w_c = [(s_pe, pT)]
                    if len(vstores) >= 2:
                        w_c.append((s_ld, vstores[-2][1]))
                    aP = act(piece[:, (tj % 2) * 128:(tj % 2) * 128 + 128],
                             ps_cb[:, slot:slot + 128], AF.Copy, waits=w_c, inc=s_act)
                    stn = ld(vtm_view[tj * 128:(tj + 1) * 128, m * 128:(m + 1) * 128],
                             piece[:, (tj % 2) * 128:(tj % 2) * 128 + 128],
                             waits=[(s_act, aP)])
                    vstores.append((aP, stn))
                    kv_st.append(stn)
                evac_dve.append(s_dve.n)
        prev_w_pe = s_pe.n
        P2_done = {"act": s_act.n, "dve": s_dve.n}

    KV_LD = s_ld.n

    E("pool", lambda e: e.collective_compute(
        "AllGather", ALU.bypass, replica_groups=GROUPS,
        ins=[kv_cin.ap().opt()], outs=[kv_out.ap().opt()]),
      waits=[(s_ld, KV_LD)], inc=s_cc)
    CC_KV = s_cc.n
    QN_DVE = s_dve.n

    ffn_cast = {}

    expb = [reg_e[:, 0:16384], reg_e[:, 16384:32768]]
    Kb = [reg_k[:, 0:4096], reg_k[:, 4096:8192]]
    Vb = [reg_k[:, 8192:12320], reg_k[:, 12320:16448]]
    VM0 = dvmemset(Vb[0], 1.0, waits=[(s_dve, P2_done["dve"]), (s_act, P2_done["act"])],
                   inc=s_dve)
    VM1 = dvmemset(Vb[1], 1.0, inc=s_dve)

    def kvload(vh, waits):
        par = vh % 2
        w = list(waits)
        for r in range(4):
            ld(Kb[par][:, r * 1024:(r + 1) * 1024],
               kv_out.ap()[r * 4097 + vh * 128:r * 4097 + (vh + 1) * 128, :],
               waits=(w if r == 0 else []))
        for r in range(4):
            src = kv_out.ap()[r * 4097 + 2048:r * 4097 + 4096, :] \
                .rearrange("(t two) c -> t (two c)", two=2)[:, vh * 128:(vh + 1) * 128] \
                .rearrange("(tc p) dv -> p tc dv", p=128)
            dstv = Vb[par][:, r * 1032:(r + 1) * 1032] \
                .rearrange("p (tc c) -> p tc c", c=129)[:, :, 0:128]
            ld(dstv, src)
        return s_ld.n

    kv_ld_n = [0] * 17
    kv_ld_n[0] = kvload(0, [(s_cc, CC_KV), (s_dve, VM0)])
    kv_ld_n[1] = kvload(1, [(s_dve, VM1)])

    phase_pe_end = []
    exp_inc = {}
    evac_acts = []
    last_pair_act = [P2_done["act"], P2_done["act"]]
    scps = [ps_a, ps_b]
    epi_dve_last = QN_DVE
    aRs_par = [0, 0]
    negrow = xn_s
    negrowB = xn_s
    mrow = logit
    NEGM_ACT = [0, 0]
    MAXST_DVE = [QN_DVE, QN_DVE]

    MAXB = ps_c[:, 1536:2048]
    MAXB_T = ps_cb[0:1, 3072:3200]

    def max_stage_steps(vh, waits):
        par = vh % 2
        w0 = list(waits)
        last_red = 0
        last_negm = 0
        for ui in range(16):
            hhp, qt = ui % 2, ui // 2
            for scq in range(8):
                w_u = []
                if ui == 0 and scq == 0:
                    w_u = list(w0)
                if last_red:
                    w_u.append((s_dve, last_red))
                if last_negm and scq == 0:
                    w_u.append((s_act, last_negm))
                pemm(MAXB[:, 0:512],
                     qbuf[hhp * 64:hhp * 64 + 64,
                          vh * 1024 + qt * 128:vh * 1024 + qt * 128 + 128],
                     Kb[par][hhp * 64:hhp * 64 + 64, scq * 512:(scq + 1) * 512],
                     waits=w_u, inc=s_pe)
                pe_u = s_pe.n
                if scq == 0:
                    last_red = dvred(mrow[:, 0:1], MAXB[:, 0:512], ALU.max,
                                     waits=[(s_pe, pe_u)], inc=s_dve)
                else:
                    dvred(mrow[:, 1:2], MAXB[:, 0:512], ALU.max,
                          waits=[(s_pe, pe_u)])
                    last_red = dvtt(mrow[:, 0:1], mrow[:, 0:1], mrow[:, 1:2],
                                    ALU.max, inc=s_dve)
                yield
            dNC = dvts(qraw[:, ui:ui + 1], mrow[:, 0:1], -1.0, None, ALU.mult,
                       inc=s_dve)
            pT = petr(MAXB_T, qraw[:, ui:ui + 1], cid_b[:, :],
                      waits=[(s_dve, dNC)], inc=s_pe)
            qh_, qt4 = qt // 4, qt % 4
            col = par * 1024 + qh_ * 512 + qt4 * 128
            dst_row = negrow[0:1, col:col + 128] if hhp == 0 else \
                negrowB[64:65, col:col + 128]
            last_negm = act(dst_row, MAXB_T, AF.Copy, waits=[(s_pe, pT)], inc=s_act)
            last_red = s_dve.n
            yield
        NEGM_ACT[par] = last_negm
        MAXST_DVE[par] = s_dve.n

    exp_last = [P2_done["act"], P2_done["act"]]
    item_pe_end = []
    for _ in max_stage_steps(0, [(s_ld, kv_ld_n[0]), (s_dve, P2_done["dve"]),
                                 (s_act, P2_done["act"])]):
        pass
    if "negm" in dbg:
        ld(dbg_d["negm"], negrow[0:2, :], waits=[(s_act, NEGM_ACT[0])])
    max_iter = [None]
    for item in range(32):
        vh, qh = item // 2, item % 2
        par = vh % 2
        if qh == 0 and vh + 1 < 16:
            war = item_pe_end[-1] if item_pe_end else 0
            kv_ld_n[vh + 1] = kvload(vh + 1, ([(s_pe, war)] if war else []))
            w_ms = [(s_ld, kv_ld_n[vh + 1])]
            if item_pe_end:
                w_ms.append((s_pe, item_pe_end[-1]))
            if item >= 2:
                w_ms.append((s_pe, item_pe_end[-1]))
            max_iter[0] = max_stage_steps(vh + 1, w_ms)
        exp_war = item_pe_end[item - 2] if item >= 2 else 0
        qoff = vh * 1024 + qh * 512
        for p in range(16):
            for half in range(2):
                sc = p * 2 + half
                for hhp in range(2):
                    w_sc = []
                    if half == 0 and hhp == 0:
                        w_sc = [(s_act, exp_last[0]), (s_act, exp_last[1])]
                        if p == 0:
                            w_sc += [(s_ld, kv_ld_n[vh]), (s_dve, MAXST_DVE[par]),
                                     (s_act, NEGM_ACT[par]), (s_dve, ONES33)]
                            if exp_war:
                                w_sc.append((s_pe, exp_war))
                    pemm(scps[hhp][:, half * 512:(half + 1) * 512],
                         Kb[par][hhp * 64:hhp * 64 + 64, sc * 128:(sc + 1) * 128],
                         qbuf[hhp * 64:hhp * 64 + 64, qoff:qoff + 512],
                         start=True, stop=False, waits=w_sc)
                if half == 1 and max_iter[0] is not None:
                    try:
                        next(max_iter[0])
                    except StopIteration:
                        max_iter[0] = None
            bcol = par * 1024 + qh * 512
            for half in range(2):
                for hhp in range(2):
                    pemm(scps[hhp][:, half * 512:(half + 1) * 512],
                         ones65[0:1, :] if hhp == 0 else ones65[64:65, :],
                         negrow[0:1, bcol:bcol + 512] if hhp == 0 else
                         negrowB[64:65, bcol:bcol + 512],
                         start=False, stop=True,
                         inc=(s_pe if (half == 1 and hhp == 1) else None))
            pe_sc = s_pe.n
            def mstep(k=1):
                if max_iter[0] is not None:
                    for _ in range(k):
                        try:
                            next(max_iter[0])
                        except StopIteration:
                            max_iter[0] = None
                            break
            mstep()
            if p >= 1:
                w_v = [(s_act, exp_inc[(0, p - 1)]), (s_act, exp_inc[(1, p - 1)])]
                if p == 1 and evac_acts:
                    w_v.append((s_act, evac_acts[-1]))
                first = True
                for half in range(2):
                    sc = (p - 1) * 2 + half
                    for qt in range(4):
                        for hhp in range(2):
                            so = NUMOFF[hhp * 4 + qt]
                            pemm(ps_c[:, so:so + 129],
                                 expb[hhp][:, sc * 512 + qt * 128:sc * 512 + qt * 128 + 128],
                                 Vb[par][:, sc * 129:(sc + 1) * 129],
                                 start=(sc == 0), stop=False,
                                 waits=(w_v if first else []))
                            first = False
                        if qt in (1, 3):
                            mstep()
            mstep()
            for hhp in range(2):
                w_e = [(s_pe, pe_sc)]
                exp_inc[(hhp, p)] = act(expb[hhp][:, p * 1024:(p + 1) * 1024],
                                        scps[hhp][:, :], AF.Exp, waits=w_e, inc=s_act)
                exp_last[hhp] = exp_inc[(hhp, p)]
        w_v = [(s_act, exp_inc[(0, 15)]), (s_act, exp_inc[(1, 15)])]
        first = True
        for half in range(2):
            sc = 30 + half
            for qt in range(4):
                for hhp in range(2):
                    so = NUMOFF[hhp * 4 + qt]
                    pemm(ps_c[:, so:so + 129],
                         expb[hhp][:, sc * 512 + qt * 128:sc * 512 + qt * 128 + 128],
                         Vb[par][:, sc * 129:(sc + 1) * 129],
                         start=False, stop=(sc == 31),
                         waits=(w_v if first else []),
                         inc=(s_pe if (half == 1 and qt == 3 and hhp == 1) else None))
                    first = False
        item_pe_end.append(s_pe.n)
        phase_pe_end.append(s_pe.n)
        if item < 16:
            kind, ee = ("fc" if item % 2 == 0 else "pj"), item // 2
            src_ = wfc_d if item % 2 == 0 else wpj_d
            dst_ = wfcb if item % 2 == 0 else wpjb
            E("pool", lambda en, s_=src_, d_=dst_, i_=ee: en.dma_start(
                out=d_.ap()[i_], in_=s_[i_]),
              waits=[(s_pe, item_pe_end[-1]), (s_ld, s_ld.n)], inc=s_gd, inck=16)
            ffn_cast[(kind, ee)] = s_gd.n
        if qh == 1 and max_iter[0] is not None:
            for _ in max_iter[0]:
                pass
            max_iter[0] = None
        aEv = act(numcp[:, :], ps_c[:, 0:1536], AF.Copy,
                  waits=[(s_pe, item_pe_end[-1]), (s_dve, epi_dve_last)], inc=s_act)
        evac_acts.append(aEv)
        if item == 0 and "numcp" in dbg:
            ld(dbg_d["numcp"], numcp[:, :], waits=[(s_act, aEv)])
        for qt in range(4):
            a0, aB, aC = attbuf[qt % 2]
            oA, oB = NUMOFF[qt], NUMOFF[4 + qt]
            dvrecip(small[:, 20:21], numcp[:, oA + 128:oA + 129], waits=[(s_act, aEv)])
            dvrecip(small[:, 21:22], numcp[:, oB + 128:oB + 129])
            dvtt(small[:, 21:22], small[:, 21:22], lamb[:, 0:1], ALU.mult)
            dvts(a0[:, :], numcp[:, oA:oA + 128], small[:, 20:21], None, ALU.mult,
                 waits=[(s_act, aRs_par[qt % 2])])
            dvts(aB[:, :], numcp[:, oB:oB + 128], small[:, 21:22], None, ALU.mult)
            dSb = dvtt(a0[:, :], a0[:, :], aB[:, :], ALU.subtract, inc=s_dve)
            act(JUNK[:, 0:128], a0[:, :], AF.Square, accum=small[:, 22:23],
                waits=[(s_dve, dSb)])
            act(small[:, 23:24], small[:, 22:23], AF.Ln, scale=1.0 / 128.0,
                bias=small[:, 8:9])
            aRs = act(small[:, 24:25], small[:, 23:24], AF.Exp, scale=-0.5, inc=s_act)
            aRs_par[qt % 2] = aRs
            dvts(aC[:, :], a0[:, :], small[:, 24:25], None, ALU.mult,
                 waits=[(s_act, aRs)])
            col = vh * 1024 + qh * 512 + qt * 128
            epi_dve_last = dvtt(reg_h[:, col:col + 128], aC[:, :], sublb[:, :],
                                ALU.mult, inc=s_dve)
    P3_PE = s_pe.n
    EPI_DVE = epi_dve_last

    tp_acts = []
    for j in range(128):
        slot = TSLOT[j % 4]
        w_t = [(s_dve, EPI_DVE)]
        if j >= 4:
            w_t = [(s_act, tp_acts[j - 4])]
        pT = petr(ps_cb[:, slot:slot + 128], reg_h[:, j * 128:(j + 1) * 128],
                  cid_b[:, :], waits=w_t, inc=s_pe)
        tp_acts.append(act(reg_h[:, j * 128:(j + 1) * 128], ps_cb[:, slot:slot + 128],
                           AF.Copy, waits=[(s_pe, pT)], inc=s_act))
    ATTNT_ACT = tp_acts[-1]
    if "attnt" in dbg:
        ld(dbg_d["attnt"], reg_h[:, :], waits=[(s_act, ATTNT_ACT)])

    for k in range(16):
        ldt(reg_e[:, k * 2048:(k + 1) * 2048], wo_bf.ap()[:, k * 128:(k + 1) * 128],
            waits=([(s_gd, GD_WO), (s_pe, P3_PE)] if k == 0 else []))
    WOT_LD = s_ld.n
    h2 = reg_q
    wo_slots = [(ps_a, 0), (ps_a, 512), (ps_b, 0), (ps_b, 512)]
    wo_evac = [0] * 4
    x1_hist = []
    h2_hist = []
    x1_st = []
    for tt in range(8):
        xst = reg_kf[:, (tt % 2) * 2048:(tt % 2) * 2048 + 2048]
        w = [(s_pe, P3_PE)]
        if tt >= 2:
            w.append((s_dve, x1_hist[tt - 2]))
        xl = ld(xst, x_d[tt * 128:(tt + 1) * 128, :], waits=w)
        x1t = reg_kf[:, 4096:6144]
        for m in range(16):
            tgt, off = wo_slots[m % 4]
            w_m = [(s_ld, WOT_LD), (s_act, ATTNT_ACT)]
            if wo_evac[m % 4]:
                w_m.append((s_dve, wo_evac[m % 4]))
            if tt > 0 and m == 0:
                w_m.append((s_dve, h2_hist[-1]))
            for k in range(16):
                pemm(tgt[:, off:off + 128],
                     reg_e[:, k * 2048 + m * 128:k * 2048 + m * 128 + 128],
                     reg_h[:, k * 1024 + tt * 128:k * 1024 + tt * 128 + 128],
                     start=(k == 0), stop=(k == 15),
                     waits=(w_m if k == 0 else []),
                     inc=(s_pe if k == 15 else None))
            mmd = s_pe.n
            dG = dvts(piece[:, (m % 2) * 128:(m % 2) * 128 + 128], tgt[:, off:off + 128],
                      camod[:, 64 + m:65 + m], None, ALU.mult,
                      waits=[(s_pe, mmd), (s_ld, xl)], inc=s_dve)
            slot = TSLOT[m % 4]
            pT = petr(ps_cb[:, slot:slot + 128], piece[:, (m % 2) * 128:(m % 2) * 128 + 128],
                      cid_b[:, :], waits=[(s_dve, dG)], inc=s_pe)
            w_x1 = [(s_pe, pT)]
            if m == 0 and x1_st:
                w_x1.append((s_ld, x1_st[-1]))
            wo_evac[m % 4] = dvtt(x1t[:, m * 128:(m + 1) * 128],
                                  xst[:, m * 128:(m + 1) * 128],
                                  ps_cb[:, slot:slot + 128], ALU.add,
                                  waits=w_x1, inc=s_dve)
        dX1 = s_dve.n
        P_ = (tt % 2) * 16
        act(xn_s[:, :], x1t[:, :], AF.Square, accum=small[:, P_ + 3:P_ + 4],
            waits=[(s_dve, dX1)])
        act(xn_s[:, :], x1t[:, :], AF.Copy, accum=small[:, P_ + 2:P_ + 3])
        aS2 = act(small[:, 10:11], small[:, 8:9], AF.Copy, inc=s_act)
        dvts(small[:, P_ + 0:P_ + 1], small[:, P_ + 2:P_ + 3], 1.0 / D, None, ALU.mult,
             waits=[(s_act, aS2)])
        dvts(small[:, P_ + 4:P_ + 5], small[:, P_ + 3:P_ + 4], 1.0 / D, None, ALU.mult)
        dvtt(small[:, P_ + 5:P_ + 6], small[:, P_ + 0:P_ + 1], small[:, P_ + 0:P_ + 1],
             ALU.mult)
        dV2 = dvtt(small[:, P_ + 6:P_ + 7], small[:, P_ + 4:P_ + 5],
                   small[:, P_ + 5:P_ + 6], ALU.subtract, inc=s_dve)
        act(small[:, P_ + 5:P_ + 6], small[:, P_ + 6:P_ + 7], AF.Ln,
            bias=small[:, 8:9], waits=[(s_dve, dV2)])
        aR2 = act(small[:, P_ + 1:P_ + 2], small[:, P_ + 5:P_ + 6], AF.Exp, scale=-0.5,
                  inc=s_act)
        dXN2 = dvts(xn_s[:, :], x1t[:, :], small[:, P_ + 0:P_ + 1],
                    small[:, P_ + 1:P_ + 2], ALU.subtract, ALU.mult,
                    waits=[(s_act, aR2)], inc=s_dve)
        ev2 = []
        for k in range(16):
            slot = TSLOT[k % 4]
            w2 = [(s_dve, dXN2)]
            if k >= 4:
                w2 = [(s_dve, ev2[k - 4])]
            pT = petr(ps_cb[:, slot:slot + 128], xn_s[:, k * 128:(k + 1) * 128],
                      cid_b[:, :], waits=w2, inc=s_pe)
            ev2.append(dvts(h2[:, k * 1024 + tt * 128:k * 1024 + tt * 128 + 128],
                            ps_cb[:, slot:slot + 128],
                            camod[:, 32 + k:33 + k], camod[:, 48 + k:49 + k],
                            ALU.mult, ALU.add, waits=[(s_pe, pT)], inc=s_dve))
        h2_hist.append(s_dve.n)
        x1_hist.append(s_dve.n)
        x1_st.append(ld(x1_dram.ap()[tt * 128:(tt + 1) * 128, :], x1t,
                        waits=[(s_dve, x1_hist[-1])]))
    H2_DVE = s_dve.n
    X1_LD = s_ld.n
    WOUT_PE = s_pe.n
    if "h2" in dbg:
        ld(dbg_d["h2"], h2[:, :], waits=[(s_dve, H2_DVE)])

    rt_done = []
    for tt in range(8):
        w_r = [(s_dve, H2_DVE)]
        if tt >= 2:
            w_r.append((s_dve, rt_done[tt - 2]))
        for k in range(16):
            pemm(ps_c[:, (tt % 2) * 512:(tt % 2) * 512 + 8],
                 h2[:, k * 1024 + tt * 128:k * 1024 + tt * 128 + 128],
                 wrtb[:, k * 8:(k + 1) * 8],
                 start=(k == 0), stop=(k == 15),
                 waits=(w_r if k == 0 else []),
                 inc=(s_pe if k == 15 else None))
        rmm = s_pe.n
        w_lg = [(s_pe, rmm)]
        if rt_done:
            w_lg.append((s_dve, rt_done[-1]))
        aLg = act(logit[:, 0:8], ps_c[:, (tt % 2) * 512:(tt % 2) * 512 + 8], AF.Copy,
                  waits=w_lg, inc=s_act)
        dvred(small[:, 25:26], logit[:, 0:8], ALU.max, waits=[(s_act, aLg)])
        dvts(logit[:, 8:16], logit[:, 0:8], small[:, 25:26], None, ALU.is_equal)
        dvts(logit[:, 16:24], logit[:, 8:16], -1e30, None, ALU.mult)
        dvtt(logit[:, 8:16], logit[:, 0:8], logit[:, 16:24], ALU.add)
        dvred(small[:, 26:27], logit[:, 8:16], ALU.max)
        dM2 = dvts(small[:, 27:28], small[:, 25:26], -1.0, None, ALU.mult, inc=s_dve)
        act(logit[:, 16:24], logit[:, 0:8], AF.Exp, bias=small[:, 27:28],
            waits=[(s_dve, dM2)])
        aE2 = act(small[:, 28:29], small[:, 26:27], AF.Exp, bias=small[:, 27:28],
                  inc=s_act)
        dvts(small[:, 30:31], small[:, 28:29], 1.0, None, ALU.add, waits=[(s_act, aE2)])
        dvrecip(small[:, 29:30], small[:, 30:31])
        dvts(logit[:, 24:32], logit[:, 0:8], small[:, 26:27], None, ALU.is_ge)
        dvtt(logit[:, 24:32], logit[:, 24:32], logit[:, 16:24], ALU.mult)
        rt_done.append(dvts(wtok[:, tt * 8:(tt + 1) * 8], logit[:, 24:32],
                            small[:, 29:30], None, ALU.mult, inc=s_dve))
    RT_DVE = rt_done[-1]
    WTR_ACT = s_act.n
    if "wtok" in dbg:
        ld(dbg_d["wtok"], wtok[:, :], waits=[(s_dve, RT_DVE)])

    ACC0 = dvmemset(reg_h[:, :], 0.0, waits=[(s_pe, WOUT_PE)], inc=s_dve)
    h2mid = reg_k[:, 0:16384]
    fcslot = [reg_e[:, 0:8192], reg_e[:, 8192:16384]]
    pjslot = [reg_e[:, 16384:24576], reg_e[:, 24576:32768]]
    RELU = strm[:, 8192:9216]
    fc_pe = []
    fc_sq = []
    pj_pe = []
    acc_dve = []
    wtb_act = 0
    for e in range(8):
        wrow6 = numcp[0:1, 1024:1152]
        wtb_act = 0
        for tt in range(8):
            w_w = [(s_dve, RT_DVE)]
            if acc_dve:
                w_w.append((s_dve, acc_dve[-1]))
            if wtb_act:
                w_w.append((s_act, wtb_act))
            pT = petr(ps_c[0:1, 1536 + (tt % 2) * 128:1536 + (tt % 2) * 128 + 128],
                      wtok[:, tt * 8 + e:tt * 8 + e + 1], cid_f[:, :],
                      waits=w_w, inc=s_pe)
            aRow = act(wrow6, ps_c[0:1, 1536 + (tt % 2) * 128:1536 + (tt % 2) * 128 + 128],
                       AF.Copy, waits=[(s_pe, pT)], inc=s_act)
            peB = pemm(ps_c[:, (tt % 2) * 512:(tt % 2) * 512 + 128], co1f[:, :], wrow6,
                       waits=[(s_act, aRow)], inc=s_pe)
            wtb_act = act(wtb[:, tt * 128:(tt + 1) * 128],
                          ps_c[:, (tt % 2) * 512:(tt % 2) * 512 + 128],
                          AF.Copy, waits=[(s_pe, peB)], inc=s_act)
        for g in range(4):
            slot = g % 2
            w_d = [(s_pe, WOUT_PE), (s_gd, ffn_cast[("fc", e)])]
            if len(fc_pe) >= 2:
                w_d = [(s_pe, fc_pe[-2]), (s_gd, ffn_cast[("fc", e)])]
            ldn = ld(fcslot[slot].rearrange("p (k d) -> p k d", d=512),
                     wfcb.ap()[e, :, g * 512:(g + 1) * 512]
                     .rearrange("(k p) d -> p k d", p=128),
                     waits=w_d)
            for dmi in range(4):
                dm = g * 4 + dmi
                tgt = ps_a if dm % 2 == 0 else ps_b
                w_f = [(s_ld, ldn), (s_dve, ACC0)]
                if len(fc_sq) >= 2:
                    w_f.append((s_act, fc_sq[-2]))
                if e > 0 and dm == 0 and pj_pe:
                    w_f.append((s_pe, pj_pe[-1]))
                for nh in range(2):
                    for k in range(16):
                        pemm(tgt[:, nh * 512:(nh + 1) * 512],
                             fcslot[slot][:, k * 512 + dmi * 128:k * 512 + dmi * 128 + 128],
                             h2[:, k * 1024 + nh * 512:k * 1024 + nh * 512 + 512],
                             start=(k == 0), stop=(k == 15),
                             waits=(w_f if (nh == 0 and k == 0) else []),
                             inc=(s_pe if (nh == 1 and k == 15) else None))
                fc_pe.append(s_pe.n)
                w_rel = [(s_pe, fc_pe[-1])]
                if fc_sq:
                    w_rel.append((s_act, fc_sq[-1]))
                dRel = dvts(RELU, tgt[:, :], 0.0, None, ALU.max, waits=w_rel, inc=s_dve)
                w_sq = [(s_dve, dRel)]
                if e > 0 and pj_pe:
                    w_sq.append((s_pe, pj_pe[-1]))
                fc_sq.append(act(h2mid[:, dm * 1024:(dm + 1) * 1024], RELU, AF.Square,
                                 waits=w_sq, inc=s_act))
        FC_ACT = fc_sq[-1]
        for g in range(4):
            slot = g % 2
            w_d = [(s_pe, WOUT_PE), (s_gd, ffn_cast[("pj", e)])]
            if len(pj_pe) >= 2:
                w_d = [(s_pe, pj_pe[-2]), (s_gd, ffn_cast[("pj", e)])]
            ldn = ld(pjslot[slot].rearrange("p (k d) -> p k d", d=512),
                     wpjb.ap()[e, :, g * 512:(g + 1) * 512]
                     .rearrange("(k p) d -> p k d", p=128),
                     waits=w_d)
            for cmi in range(4):
                cm = g * 4 + cmi
                tps = ps_c[:, 0:1024] if cm % 2 == 0 else ps_c[:, 1024:2048]
                w_p = [(s_ld, ldn), (s_act, FC_ACT), (s_act, wtb_act)]
                if len(acc_dve) >= 2:
                    w_p.append((s_dve, acc_dve[-2]))
                for nh in range(2):
                    for dm in range(16):
                        pemm(tps[:, nh * 512:(nh + 1) * 512],
                             pjslot[slot][:, dm * 512 + cmi * 128:dm * 512 + cmi * 128 + 128],
                             h2mid[:, dm * 1024 + nh * 512:dm * 1024 + nh * 512 + 512],
                             start=(dm == 0), stop=(dm == 15),
                             waits=(w_p if (nh == 0 and dm == 0) else []),
                             inc=(s_pe if (nh == 1 and dm == 15) else None))
                pj_pe.append(s_pe.n)
                dT = dvtt(scr1[:, :], tps[:, :], wtb[:, :], ALU.mult,
                          waits=[(s_pe, pj_pe[-1])])
                acc_dve.append(dvtt(reg_h[:, cm * 1024:(cm + 1) * 1024],
                                    reg_h[:, cm * 1024:(cm + 1) * 1024], scr1[:, :],
                                    ALU.add, inc=s_dve))
    P6_DVE = acc_dve[-1]
    P6_PE = s_pe.n
    if "acc" in dbg:
        ld(dbg_d["acc"], reg_h[:, :], waits=[(s_dve, P6_DVE)])

    GB_DVE = P6_DVE
    out_dve = []
    pg_hist = []
    for tt in range(8):
        xst = reg_kf[:, 0:2048]
        yst = reg_kf[:, 2048:4096]
        w = [(s_pe, P6_PE)]
        if tt > 0:
            w.append((s_out, tt * 16))
        xl = ld(xst, x1_dram.ap()[tt * 128:(tt + 1) * 128, :], waits=w)
        for cm in range(16):
            slot = TSLOT[cm % 4]
            w_g = [(s_dve, GB_DVE), (s_pe, P6_PE)]
            if len(out_dve) >= 2:
                w_g.append((s_pe, pg_hist[-2]))
            dg = dvts(piece[:, (cm % 2) * 128:(cm % 2) * 128 + 128],
                      reg_h[:, cm * 1024 + tt * 128:cm * 1024 + tt * 128 + 128],
                      camod[:, 80 + cm:81 + cm], None, ALU.mult,
                      waits=w_g, inc=s_dve)
            w_t = [(s_dve, dg)]
            if len(out_dve) >= 4:
                w_t.append((s_dve, out_dve[-4]))
            pT = petr(ps_cb[:, slot:slot + 128],
                      piece[:, (cm % 2) * 128:(cm % 2) * 128 + 128],
                      cid_b[:, :], waits=w_t, inc=s_pe)
            pg_hist.append(pT)
            out_dve.append(dvtt(yst[:, cm * 128:(cm + 1) * 128],
                                xst[:, cm * 128:(cm + 1) * 128],
                                ps_cb[:, slot:slot + 128], ALU.add,
                                waits=[(s_pe, pT), (s_ld, xl)], inc=s_dve))
        E("syncq", lambda e, t=tt, y=yst: e.dma_start(
            out=y_d[t * 128:(t + 1) * 128, :], in_=y),
          waits=[(s_dve, out_dve[-1])], inc=s_out, inck=16)

    if "modv" in dbg:
        ld(dbg_d["modv"], modv[:, :], waits=[(s_act, aMV)])
    if "camod" in dbg:
        ld(dbg_d["camod"], camod[:, :], waits=[(s_dve, DVE_MOD)])
    if "lamb" in dbg:
        ld(dbg_d["lamb"], lamb[:, :], waits=[(s_dve, DVE_MOD)])
    if False and "h1" in dbg:
        ld(dbg_d["h1"], h1[:, :], waits=[(s_dve, P1_DVE)])
    if "qrot" in dbg:
        ld(dbg_d["qrot"], qbuf[:, :], waits=[(s_dve, P2_done["dve"])])
    if "kcin" in dbg:
        ld(dbg_d["kcin"], kv_cin.ap()[:, :], waits=[(s_ld, KV_LD)])
    if False and "attnt" in dbg:
        ld(dbg_d["attnt"], reg_h[:, :], waits=[(s_act, ATTNT_ACT)])
    if "x1" in dbg:
        ld(dbg_d["x1"], x1_dram.ap()[:, :], waits=[(s_ld, X1_LD)])
    if False and "h2" in dbg:
        ld(dbg_d["h2"], h2[:, :], waits=[(s_dve, H2_DVE)])
    if False and "wtok" in dbg:
        ld(dbg_d["wtok"], wtok[:, :], waits=[(s_dve, RT_DVE)])
    if False and "acc" in dbg:
        ld(dbg_d["acc"], reg_h[:, :], waits=[(s_dve, P6_DVE)])

    E("syncq", lambda e: e.nop(), waits=[(s_out, s_out.n), (s_ld, s_ld.n)])

    blk = stk.enter_context(nc.Block())

    @blk.tensor
    def _(t):
        for f in sch.ops["pe"]:
            f(t)

    @blk.scalar
    def _(s):
        for f in sch.ops["act"]:
            f(s)

    @blk.vector
    def _(v):
        for f in sch.ops["dve"]:
            f(v)

    @blk.gpsimd
    def _(g):
        for f in sch.ops["pool"]:
            f(g)

    @blk.sync
    def _(sy):
        for f in sch.ops["syncq"]:
            f(sy)

    stk.close()
    nc._sched_trace = sch
    return nc, dbg_d


_BUILT = {}


def _get_program(dbg=()):
    key = tuple(sorted(dbg))
    if key not in _BUILT:
        _BUILT[key] = build_program(dbg)
    return _BUILT[key]


def make_inmaps(inputs):
    lams = np.concatenate([
        np.asarray(inputs["L0_attn_lambda_q1"]), np.asarray(inputs["L0_attn_lambda_k1"]),
        np.asarray(inputs["L0_attn_lambda_q2"]), np.asarray(inputs["L0_attn_lambda_k2"])]
    ).reshape(1, -1).astype(np.float32)
    x = np.asarray(inputs["L0_attn_x"])
    wmod = np.asarray(inputs["L0_wmod"])
    c = np.asarray(inputs["L0_c"])
    in_maps = []
    for core in range(NCORES):
        b, tq = core // 4, core % 4
        m = dict(_host_consts(core))
        m["x_sl"] = np.ascontiguousarray(x[b, tq * TC:(tq + 1) * TC, :])
        m["wq"] = np.asarray(inputs["L0_attn_wq"])
        m["wk"] = np.asarray(inputs["L0_attn_wk"])
        m["wv"] = np.asarray(inputs["L0_attn_wv"])
        m["wout"] = np.asarray(inputs["L0_attn_wout"])
        m["wfc"] = np.asarray(inputs["L0_ffn_w_fc"])
        m["wproj"] = np.asarray(inputs["L0_ffn_w_proj"])
        m["wrouter"] = np.asarray(inputs["L0_ffn_w_router"])
        m["wmod_sl"] = np.ascontiguousarray(wmod[:, tq * MODW:(tq + 1) * MODW])
        m["c2"] = np.ascontiguousarray(c[[b, 1 - b], :])
        m["lams"] = lams
        m["subln"] = np.asarray(inputs["L0_attn_subln_weight"]).reshape(1, -1).astype(np.float32)
        in_maps.append(m)
    return in_maps


def kernel(**inputs):
    nc, _ = _get_program()
    in_maps = make_inmaps(inputs)
    res = run_bass_kernel_spmd(nc, in_maps, core_ids=list(range(NCORES)))
    out = np.empty((B, T, D), np.float32)
    for core in range(NCORES):
        b, tq = core // 4, core % 4
        out[b, tq * TC:(tq + 1) * TC, :] = res.results[core]["y"]
    return out


# version 12
# speedup vs baseline: 1.0285x; 1.0285x over previous
import numpy as np
import ml_dtypes
import concourse.bass as bass
import concourse.mybir as mybir
from concourse.bass_utils import run_bass_kernel_spmd

BF = ml_dtypes.bfloat16
F32 = mybir.dt.float32
BF16 = mybir.dt.bfloat16
AF = mybir.ActivationFunctionType
ALU = mybir.AluOpType
AX = mybir.AxisListType

B, T, D = 2, 4096, 2048
TC = 1024
NCORES = 8
HALF = 64
SCALE = float(128.0 ** 0.5)
LAMBDA_INIT = 0.8 - 0.6 * float(np.exp(-0.3 * 12))
EPS = 1e-5
MODW = 3072
GROUPS = [[0, 1, 2, 3], [4, 5, 6, 7]]
ROPE0 = 8192
NUMOFF = [(s // 3) * 512 + (s % 3) * 129 for s in range(8)]
TSLOT = [0, 1024, 2048, 3072]


def _rope_tables():
    inv = 1.0 / 10000.0 ** (np.arange(0, HALF, 2)[: HALF // 2].astype(np.float64) / HALF)
    ang = np.outer(np.arange(T), inv)
    return np.cos(ang).astype(np.float32), np.sin(ang).astype(np.float32)


def _host_consts(core):
    tq = core % 4
    cos, sin = _rope_tables()
    sl = slice(tq * TC, (tq + 1) * TC)
    c, s = cos[sl].T, sin[sl].T
    idx = (np.arange(128) % HALF) // 2
    Crow = c[idx]
    Srow = s[idx] * np.where(np.arange(128) % 2 == 0, -1.0, 1.0)[:, None]
    psw = np.zeros((128, 128), np.float32)
    i = np.arange(0, 128, 2)
    psw[i, i + 1] = 1.0
    psw[i + 1, i] = 1.0
    return {
        "ropecq": np.ascontiguousarray((Crow * SCALE).astype(BF)),
        "ropesq": np.ascontiguousarray((Srow * SCALE).astype(BF)),
        "ropeck": np.ascontiguousarray(Crow.astype(BF)),
        "ropesk": np.ascontiguousarray(Srow.astype(BF)),
        "pswap": psw.astype(BF),
        "identb": np.eye(128, dtype=np.float32).astype(BF),
        "identf": np.eye(128, dtype=np.float32),
        "ones128b": np.ones((128, 1), np.float32).astype(BF),
        "ones1f": np.ones((1, 128), np.float32),
        "ones1b": np.ones((1, 128), np.float32).astype(BF),
    }


class _S:
    def __init__(self, handle, name):
        self.h = handle
        self.name = name
        self.n = 0


class _Sched:
    QUEUES = ("pe", "act", "dve", "pool", "syncq")

    def __init__(self):
        self.ops = {q: [] for q in self.QUEUES}
        self.guar = {q: {} for q in self.QUEUES}
        self.trace = []
        self.sdve = None
        self.dve_pend = 0

    def emit(self, q, builder, waits=(), inc=None, inck=1, dvw=None):
        if q == "dve" and self.dve_pend:
            waits = list(waits) + [(self.sdve, self.dve_pend)]
            self.dve_pend = 0
        g = self.guar[q]
        baked = []
        rec_w = []
        for (s, v) in waits:
            rec_w.append((s.name, v))
            if v > 0 and g.get(s.name, 0) < v:
                baked.append((s.h, v))
                g[s.name] = v
        if q == "dve" and dvw is not None and dvw < 64 and inc is None:
            inc = self.sdve
        self.trace.append((q, tuple(rec_w), (inc.name if inc else None), inck,
                           getattr(builder, "__name__", "op")))
        if inc is not None:
            inc.n += inck
        if q == "dve" and dvw is not None and dvw < 64:
            self.dve_pend = inc.n
        def f(eng, baked=baked, builder=builder, ih=(inc.h if inc else None), ik=inck):
            for h, v in baked:
                eng.wait_ge(h, v)
            ins = builder(eng)
            if ih is not None:
                ins.then_inc(ih, ik)
        self.ops[q].append(f)


def build_program(dbg=()):
    nc = bass.Bass("TRN2", target_bir_lowering=False)
    dbg = set(dbg)

    dti = lambda name, shape, dty=F32: nc.dram_tensor(name, shape, dty, kind="ExternalInput").ap()
    x_d = dti("x_sl", [TC, D])
    wq_d = dti("wq", [D, D]); wk_d = dti("wk", [D, D]); wv_d = dti("wv", [D, D]); wo_d = dti("wout", [D, D])
    wfc_d = dti("wfc", [8, D, D]); wpj_d = dti("wproj", [8, D, D])
    wrt_d = dti("wrouter", [D, 8])
    wmod_d = dti("wmod_sl", [D, MODW])
    c2_d = dti("c2", [2, D])
    lam_d = dti("lams", [1, 4 * HALF])
    sub_d = dti("subln", [1, 128])
    cq_d = dti("ropecq", [128, TC], BF16); sq_d = dti("ropesq", [128, TC], BF16)
    ck_d = dti("ropeck", [128, TC], BF16); sk_d = dti("ropesk", [128, TC], BF16)
    psw_d = dti("pswap", [128, 128], BF16)
    idb_d = dti("identb", [128, 128], BF16)
    idf_d = dti("identf", [128, 128], F32)
    o128_d = dti("ones128b", [128, 1], BF16)
    o1f_d = dti("ones1f", [1, 128], F32)
    o1b_d = dti("ones1b", [1, 128], BF16)

    y_d = nc.dram_tensor("y", [TC, D], F32, kind="ExternalOutput").ap()
    DBG_MENU = {
        "h1": ([128, 16384], BF16), "qrot": ([128, 16384], BF16),
        "kcin": ([4097, TC], BF16), "modv": ([128, 96], F32),
        "camod": ([128, 96], F32), "negcb": ([128, 32], F32),
        "lamb": ([128, 1], F32), "attnt": ([128, 16384], BF16),
        "x1": ([TC, D], F32), "h2": ([128, 16384], BF16),
        "wtok": ([128, 64], F32), "acc": ([128, 16384], BF16),
        "nstat": ([1, 128], F32), "ltile": ([128, 256], BF16),
        "negm": ([2, 2048], BF16), "mrow": ([128, 32], F32),
        "numcp": ([128, 1536], F32),
        "lraw": ([1, 256], BF16), "lsmall": ([128, 32], F32),
        "p1small": ([128, 32], F32), "p1xn": ([128, 2048], BF16),
    }
    dbg_d = {n: nc.dram_tensor("dbg_" + n, DBG_MENU[n][0], DBG_MENU[n][1],
                               kind="ExternalOutput").ap() for n in dbg}

    wq_bf = nc.dram_tensor("wq_bf", [D, D], BF16)
    wk_bf = nc.dram_tensor("wk_bf", [D, D], BF16)
    wv_bf = nc.dram_tensor("wv_bf", [D, D], BF16)
    wo_bf = nc.dram_tensor("wo_bf", [D, D], BF16)
    wfcb = nc.dram_tensor("wfcb", [8, D, D], BF16)
    wpjb = nc.dram_tensor("wpjb", [8, D, D], BF16)
    wmod_bf = nc.dram_tensor("wmod_bf", [D, MODW], BF16)
    kv_cin = nc.dram_tensor("kv_cin", [4097, TC], BF16)
    kv_out = nc.dram_tensor("kv_out", [4 * 4097, TC], BF16)
    mod_cin = nc.dram_tensor("mod_cin", [1, MODW], F32)
    mod_out = nc.dram_tensor("mod_out", [4, MODW], F32)
    x1_dram = nc.dram_tensor("x1_dram", [TC, D], F32)

    from contextlib import ExitStack
    stk = ExitStack()
    sb = lambda name, shape, dty: stk.enter_context(nc.sbuf_tensor(name, shape, dty))
    psb = lambda name, shape, dty: stk.enter_context(nc.psum_tensor(name, shape, dty))

    reg_h = sb("reg_h", [128, 16384], BF16)
    reg_q = sb("reg_q", [128, 16384], BF16)
    reg_e = sb("reg_e", [128, 32768], BF16)
    reg_k = sb("reg_k", [128, 16448], BF16)
    strm = sb("strm", [128, 9216], BF16)
    scr1 = sb("scr1", [128, 1024], F32)
    scr2 = sb("scr2", [128, 1024], F32)
    numcp = sb("numcp", [128, 1536], F32)
    xn_s = sb("xn_s", [128, 2048], BF16)
    qraw = sb("qraw", [128, 1024], BF16)
    small = sb("small", [128, 32], F32)
    modv = sb("modv", [128, 96], F32)
    camod = sb("camod", [128, 96], F32)
    negcb = sb("negcb", [128, 32], F32)
    lamb = sb("lamb", [128, 1], F32)
    sublb = sb("sublb", [128, 128], F32)
    wrtb = sb("wrtb", [128, 128], BF16)
    wtok = sb("wtok", [128, 64], F32)
    logit = sb("logit", [128, 32], F32)
    nstat = sb("nstat", [1, 128], F32)
    k2loc = sb("k2loc", [1, 32], F32)
    k2bf = sb("k2bf", [1, 128], BF16)
    modm = sb("modm", [96, 128], F32)
    cid_b = sb("cid_b", [128, 128], BF16)
    cid_f = sb("cid_f", [128, 128], F32)
    cpsw = sb("cpsw", [128, 128], BF16)
    co128 = sb("co128", [128, 1], BF16)
    co1f = sb("co1f", [1, 128], F32)
    co1b = sb("co1b", [1, 128], BF16)
    silct = sb("silct", [128, 32], BF16)
    piece = sb("piece", [128, 256], BF16)
    ones65 = sb("ones65", [65, 128], BF16)
    attbuf = [(scr2[:, 0:128], scr2[:, 128:256], scr2[:, 256:384]),
              (scr2[:, 384:512], scr2[:, 512:640], scr2[:, 640:768])]
    wtb = scr2[:, 0:1024]
    csilb = reg_e[0:2, 4096:6144]
    ltile_t = sb("ltile", [128, 256], BF16)
    ltile = ltile_t[:, :]
    lraw_t = sb("lraw", [1, 256], BF16)
    lraw = lraw_t[:, :]
    sraw_t = sb("sraw", [1, 128], F32)
    sraw = sraw_t[:, :]

    ps_a = psb("ps_a", [128, 1024], F32)
    ps_b = psb("ps_b", [128, 1024], F32)
    ps_c = psb("ps_c", [128, 2048], F32)
    ps_cb = ps_c.bitcast(BF16)
    ps_ab = ps_a.bitcast(BF16)

    sem = lambda name: _S(stk.enter_context(nc.semaphore(name)), name)
    s_ld = sem("s_ld"); s_gd = sem("s_gd")
    s_pe = sem("s_pe"); s_act = sem("s_act"); s_dve = sem("s_dve")
    s_cc = sem("s_cc"); s_out = sem("s_out")

    sch = _Sched()
    sch.sdve = s_dve
    E = sch.emit

    def ld(dst_ap, src_ap, q="syncq", waits=()):
        s = s_ld if q == "syncq" else s_gd
        E(q, lambda e, d=dst_ap, sr=src_ap: e.dma_start(out=d, in_=sr),
          waits=waits, inc=s, inck=16)
        return s.n

    def ldt(dst_ap, src_ap, waits=()):
        E("syncq", lambda e, d=dst_ap, sr=src_ap: e.dma_start(out=d, in_=sr, transpose=True),
          waits=waits, inc=s_ld, inck=16)
        return s_ld.n

    def pemm(out_ap, l_ap, r_ap, start=True, stop=True, waits=(), inc=None):
        E("pe", lambda e, o=out_ap, l=l_ap, r=r_ap, st=start, sp=stop:
          e.matmul(o, l, r, start=st, stop=sp), waits=waits, inc=inc)
        return s_pe.n

    def petr(out_ap, in_ap, ident_ap, waits=(), inc=None):
        E("pe", lambda e, o=out_ap, i=in_ap, d=ident_ap: e.transpose(o, i, d),
          waits=waits, inc=inc)
        return s_pe.n

    def act(out_ap, in_ap, func, bias=None, scale=1.0, accum=None, waits=(), inc=None):
        def f(e, o=out_ap, i=in_ap, fn=func, b=bias, s=scale, a=accum):
            kw = dict(func=fn, scale=s)
            if b is not None:
                kw["bias"] = b
            if a is not None:
                kw["accum_out"] = a
            return e.activation(out=o, in_=i, **kw)
        E("act", f, waits=waits, inc=inc)
        return s_act.n

    def _w(ap):
        try:
            sh = list(ap.shape)
        except Exception:
            return None
        n = 1
        for d in sh[1:]:
            n *= int(d)
        return n

    def dvts(out_ap, in_ap, sc1, sc2, op0, op1=None, waits=(), inc=None):
        def f(e, o=out_ap, i=in_ap, a=sc1, b=sc2, p0=op0, p1=op1):
            if p1 is None:
                return e.tensor_scalar(o, i, a, None, p0)
            return e.tensor_scalar(o, i, a, b, p0, p1)
        E("dve", f, waits=waits, inc=inc, dvw=_w(out_ap))
        return s_dve.n

    def dvtt(out_ap, a_ap, b_ap, op, waits=(), inc=None):
        E("dve", lambda e, o=out_ap, a=a_ap, b=b_ap, p=op: e.tensor_tensor(o, a, b, p),
          waits=waits, inc=inc, dvw=_w(out_ap))
        return s_dve.n

    def dvred(out_ap, in_ap, op, waits=(), inc=None):
        E("dve", lambda e, o=out_ap, i=in_ap, p=op: e.tensor_reduce(o, i, AX.X, p),
          waits=waits, inc=inc, dvw=_w(out_ap))
        return s_dve.n

    def dvcp(out_ap, in_ap, waits=(), inc=None):
        E("dve", lambda e, o=out_ap, i=in_ap: e.tensor_copy(o, i), waits=waits, inc=inc,
          dvw=_w(out_ap))
        return s_dve.n

    def dvrecip(out_ap, in_ap, waits=(), inc=None):
        E("dve", lambda e, o=out_ap, i=in_ap: e.reciprocal(o, i), waits=waits, inc=inc,
          dvw=_w(out_ap))
        return s_dve.n

    def dvmemset(ap, val, waits=(), inc=None):
        E("dve", lambda e, a=ap, v=val: e.memset(a, v), waits=waits, inc=inc,
          dvw=_w(ap))
        return s_dve.n

    ld(cid_b[:, :], idb_d)
    ld(cid_f[:, :], idf_d)
    ld(cpsw[:, :], psw_d)
    ld(co128[:, :], o128_d)
    ld(co1f[:, :], o1f_d)
    ld(co1b[:, :], o1b_d)
    ld(reg_k[:, ROPE0:ROPE0 + 1024], cq_d)
    ld(reg_k[:, ROPE0 + 1024:ROPE0 + 2048], sq_d)
    ld(reg_k[:, ROPE0 + 2048:ROPE0 + 3072], ck_d)
    ld(reg_k[:, ROPE0 + 3072:ROPE0 + 4096], sk_d)
    ld(lraw, lam_d, q="pool")
    ld(wrtb[:, :].rearrange("p (k e) -> p k e", e=8),
       wrt_d.rearrange("(k p) e -> p k e", p=128), q="pool")
    ld(sraw, sub_d)
    ld(csilb, c2_d, q="pool")
    E("pool", lambda e: e.dma_start(out=wmod_bf.ap()[:, :], in_=wmod_d),
      inc=s_gd, inck=16)
    GD_WMOD = s_gd.n
    P0_LD = s_ld.n
    P0_GD = s_gd.n
    dvmemset(small[:, 8:9], EPS)
    ONES33 = dvmemset(ones65[:, :], 1.0, inc=s_dve)

    peL = pemm(ps_c[:, 0:256], co1b[:, :], lraw,
               waits=[(s_ld, P0_LD), (s_gd, P0_GD)], inc=s_pe)
    aL = act(ltile, ps_c[:, 0:256], AF.Copy, waits=[(s_pe, peL)], inc=s_act)
    dvtt(scr1[:, 0:64], ltile[:, 0:64], ltile[:, 64:128], ALU.mult, waits=[(s_act, aL)])
    dvred(small[:, 16:17], scr1[:, 0:64], ALU.add)
    dvtt(scr1[:, 0:64], ltile[:, 128:192], ltile[:, 192:256], ALU.mult)
    dL = dvred(small[:, 17:18], scr1[:, 0:64], ALU.add, inc=s_dve)
    act(small[:, 18:19], small[:, 16:17], AF.Exp, waits=[(s_dve, dL)])
    aL2 = act(small[:, 19:20], small[:, 17:18], AF.Exp, inc=s_act)
    dvtt(small[:, 20:21], small[:, 18:19], small[:, 19:20], ALU.subtract,
         waits=[(s_act, aL2)])
    dvts(lamb[:, 0:1], small[:, 20:21], LAMBDA_INIT, None, ALU.add, inc=s_dve)

    if "lraw" in dbg:
        ld(dbg_d["lraw"], lraw, waits=[(s_gd, P0_GD)])
    if "ltile" in dbg:
        ld(dbg_d["ltile"], ltile, waits=[(s_act, aL)])
    if "lsmall" in dbg:
        LAMD = s_dve.n
        ld(dbg_d["lsmall"], small[:, :], waits=[(s_dve, LAMD)])
    peS = pemm(ps_c[:, 1024:1152], co1f[:, :], sraw, waits=[(s_act, aL)], inc=s_pe)
    aS = act(sublb[:, :], ps_c[:, 1024:1152], AF.Copy, waits=[(s_pe, peS)], inc=s_act)

    aSilb = act(csilb, csilb, AF.Silu,
                waits=[(s_ld, P0_LD), (s_gd, P0_GD)], inc=s_act)
    tr_done = []
    for k in range(16):
        slot = TSLOT[k % 4] + 512
        w = [(s_act, aSilb), (s_pe, peS)]
        if k >= 4:
            w.append((s_act, tr_done[k - 4]))
        pk = petr(ps_cb[:, slot:slot + 2], csilb[0:2, k * 128:(k + 1) * 128], cid_b[0:2, 0:2],
                  waits=w, inc=s_pe)
        tr_done.append(act(silct[:, 2 * k:2 * k + 2], ps_cb[:, slot:slot + 2], AF.Copy,
                           waits=[(s_pe, pk)], inc=s_act))
    aSilt = tr_done[-1]
    mod_slices = [(ps_a, 0), (ps_a, 512), (ps_b, 0), (ps_b, 512), (ps_c, 0), (ps_c, 512)]
    mod_pe = []
    for k in range(16):
        slot = (k % 2) * 3072
        w = [(s_gd, GD_WMOD)]
        if k >= 2:
            w.append((s_pe, mod_pe[k - 2]))
        gk = ld(strm[:, slot:slot + 3072], wmod_bf.ap()[k * 128:(k + 1) * 128, :],
                waits=w)
        for nb in range(6):
            tgt, off = mod_slices[nb]
            pemm(tgt[0:2, off:off + 512], silct[:, 2 * k:2 * k + 2],
                 strm[:, slot + nb * 512:slot + (nb + 1) * 512],
                 start=(k == 0), stop=(k == 15),
                 waits=([(s_ld, gk), (s_act, aSilt)] if nb == 0 else []),
                 inc=(s_pe if nb == 5 else None))
        mod_pe.append(s_pe.n)
    mod_ld = 0
    for nb in range(6):
        tgt, off = mod_slices[nb]
        w_ev = [(s_pe, mod_pe[-1])]
        if nb > 0:
            w_ev.append((s_gd, mod_ld))
        aM = act(scr1[0:1, 0:512], tgt[0:1, off:off + 512], AF.Copy,
                 waits=w_ev, inc=s_act)
        mod_ld = ld(mod_cin.ap()[:, nb * 512:(nb + 1) * 512], scr1[0:1, 0:512],
                    q="pool", waits=[(s_act, aM)])
    E("pool", lambda e: e.collective_compute(
        "AllGather", ALU.bypass, replica_groups=GROUPS,
        ins=[mod_cin.ap().opt()], outs=[mod_out.ap().opt()]),
      waits=[(s_gd, mod_ld)], inc=s_cc)
    CC_MOD = s_cc.n
    mm_ld = ld(modm[:, :], mod_out.ap().rearrange("r (g p) -> (r g) p", p=128),
               q="pool", waits=[(s_cc, CC_MOD)])
    peMV = petr(ps_c[:, 1024:1120], modm[:, :], cid_f[0:96, 0:96],
                waits=[(s_gd, mm_ld), (s_act, aS)], inc=s_pe)
    aMV = act(modv[:, :], ps_c[:, 1024:1120], AF.Copy, waits=[(s_pe, peMV)], inc=s_act)
    dvts(camod[:, 0:16], modv[:, 16:32], 1.0, None, ALU.add, waits=[(s_act, aMV)])
    dvcp(camod[:, 16:32], modv[:, 0:16])
    dvts(camod[:, 32:48], modv[:, 64:80], 1.0, None, ALU.add)
    dvcp(camod[:, 48:64], modv[:, 48:64])
    for (vsrc, vdst) in ((32, 64), (80, 80)):
        dvts(scr1[:, 0:16], modv[:, vsrc:vsrc + 16], -1.0, None, ALU.mult)
        dvtt(scr1[:, 16:32], modv[:, vsrc:vsrc + 16], scr1[:, 0:16], ALU.max)
        dvts(scr1[:, 32:48], scr1[:, 16:32], 0.25, None, ALU.max)
        dvts(scr1[:, 48:64], modv[:, vsrc:vsrc + 16], 0.0, None, ALU.is_ge)
        dvts(scr1[:, 0:16], scr1[:, 48:64], 2.0, 1.0, ALU.mult, ALU.subtract)
        dvtt(camod[:, vdst:vdst + 16], scr1[:, 32:48], scr1[:, 0:16], ALU.mult)
    DVE_MOD = dvmemset(scr1[:, 63:64], 0.0, inc=s_dve)

    reg_kf = reg_k.bitcast(F32)
    h1 = reg_h
    qbuf = reg_q
    JUNK = strm[:, 8192:9216]
    p1_evac = []
    for tt in range(8):
        xst = reg_kf[:, (tt % 2) * 2048:(tt % 2) * 2048 + 2048]
        w = [(s_dve, p1_evac[tt - 2])] if tt >= 2 else []
        xl = ld(xst, x_d[tt * 128:(tt + 1) * 128, :], waits=w)
        P_ = (tt % 2) * 16
        act(reg_e[:, 0:2048], xst, AF.Square, accum=small[:, P_ + 3:P_ + 4],
            waits=[(s_ld, xl)])
        act(reg_e[:, 2048:4096], xst, AF.Copy, accum=small[:, P_ + 2:P_ + 3])
        aST = act(small[:, 10:11], small[:, 8:9], AF.Copy, inc=s_act)
        dvts(small[:, P_ + 0:P_ + 1], small[:, P_ + 2:P_ + 3], 1.0 / D, None, ALU.mult,
             waits=[(s_act, aST)])
        dvts(small[:, P_ + 4:P_ + 5], small[:, P_ + 3:P_ + 4], 1.0 / D, None, ALU.mult)
        dvtt(small[:, P_ + 5:P_ + 6], small[:, P_ + 0:P_ + 1], small[:, P_ + 0:P_ + 1],
             ALU.mult)
        dV = dvtt(small[:, P_ + 6:P_ + 7], small[:, P_ + 4:P_ + 5],
                  small[:, P_ + 5:P_ + 6], ALU.subtract, inc=s_dve)
        act(small[:, P_ + 5:P_ + 6], small[:, P_ + 6:P_ + 7], AF.Ln,
            bias=small[:, 8:9], waits=[(s_dve, dV)])
        aR = act(small[:, P_ + 1:P_ + 2], small[:, P_ + 5:P_ + 6], AF.Exp, scale=-0.5,
                 inc=s_act)
        dXN = dvts(xn_s[:, :], xst, small[:, P_ + 0:P_ + 1], small[:, P_ + 1:P_ + 2],
                   ALU.subtract, ALU.mult, waits=[(s_act, aR)], inc=s_dve)
        evac_hist = []
        for k in range(16):
            slot = TSLOT[2 + (k % 2)]
            w2 = [(s_dve, dXN), (s_act, aMV)]
            if k >= 2:
                w2.append((s_dve, evac_hist[k - 2]))
            pT = petr(ps_cb[:, slot:slot + 128], xn_s[:, k * 128:(k + 1) * 128],
                      cid_b[:, :], waits=w2, inc=s_pe)
            evac_hist.append(
                dvts(h1[:, k * 1024 + tt * 128:k * 1024 + tt * 128 + 128],
                     ps_cb[:, slot:slot + 128],
                     camod[:, k:k + 1], camod[:, 16 + k:16 + k + 1], ALU.mult, ALU.add,
                     waits=[(s_pe, pT), (s_dve, DVE_MOD)], inc=s_dve))
        p1_evac.append(s_dve.n)
        P1_ACT = aST
        if tt == 7 and "h1" in dbg:
            ld(dbg_d["h1"], h1[:, :], waits=[(s_dve, s_dve.n)])
        if tt == 7 and "p1small" in dbg:
            ld(dbg_d["p1small"], small[:, :], waits=[(s_dve, dXN)])
        if tt == 7 and "p1xn" in dbg:
            ld(dbg_d["p1xn"], xn_s[:, :], waits=[(s_dve, dXN)])
    P1_DVE = s_dve.n

    CQ = reg_k[:, ROPE0:ROPE0 + 1024]
    SQ = reg_k[:, ROPE0 + 1024:ROPE0 + 2048]
    CK = reg_k[:, ROPE0 + 2048:ROPE0 + 3072]
    SK = reg_k[:, ROPE0 + 3072:ROPE0 + 4096]
    kst = [reg_k[:, 12288:13312], reg_k[:, 13312:14336]]
    vtm_view = kv_cin.ap()[2048:4096, :].rearrange("(t two) c -> t (two c)", two=2)

    for (src, dst) in ((wq_d, wq_bf), (wk_d, wk_bf), (wv_d, wv_bf), (wo_d, wo_bf)):
        E("pool", lambda e, s=src, d=dst: e.dma_start(out=d.ap()[:, :], in_=s),
          inc=s_gd, inck=16)
    GD_WO = s_gd.n
    GD_W = {"q": GD_WO - 3, "k": GD_WO - 2, "v": GD_WO - 1}

    prev_w_pe = 0
    P2_done = {"act": 0, "dve": 0}
    kv_st = []
    for wi, wname in enumerate(("q", "k", "v")):
        wbf = (wq_bf, wk_bf, wv_bf)[wi]
        wts = [(s_gd, GD_W[wname])] + ([(s_pe, prev_w_pe)] if wi else
                                             [(s_act, P1_ACT), (s_act, aSilt),
                                              (s_pe, mod_pe[-1])])
        for k in range(16):
            ldt(reg_e[:, k * 2048:(k + 1) * 2048], wbf.ap()[:, k * 128:(k + 1) * 128],
                waits=(wts if k == 0 else []))
        trd = s_ld.n
        evac_act = []
        evac_dve = []
        vstores = []
        for m in range(16):
            tgt = ps_a if m % 2 == 0 else ps_b
            w_p = [(s_ld, trd)]
            if m >= 2:
                w_p.append((s_act, evac_act[m - 2]))
            elif wi > 0:
                w_p.append((s_act, P2_done["act"]))
            if wi == 0 and m == 0:
                w_p.append((s_dve, P1_DVE))
            for nh in range(2):
                for k in range(16):
                    pemm(tgt[:, nh * 512:(nh + 1) * 512],
                         reg_e[:, k * 2048 + m * 128:k * 2048 + m * 128 + 128],
                         h1[:, k * 1024 + nh * 512:k * 1024 + nh * 512 + 512],
                         start=(k == 0), stop=(k == 15),
                         waits=(w_p if (nh == 0 and k == 0) else []),
                         inc=(s_pe if (nh == 1 and k == 15) else None))
            mmd = s_pe.n
            w_e = [(s_pe, mmd)]
            if evac_dve:
                w_e.append((s_dve, evac_dve[-1]))
            elif wi > 0:
                w_e.append((s_dve, P2_done["dve"]))
            aQ = act(qraw[:, :], tgt[:, :], AF.Copy, waits=w_e, inc=s_act)
            evac_act.append(aQ)
            if wname in ("q", "k"):
                coff = (m % 2) * 1024
                w_s = [(s_act, aQ)]
                if m >= 2:
                    w_s.append((s_dve, evac_dve[m - 2]))
                elif wi > 0:
                    w_s.append((s_dve, P2_done["dve"]))
                for nh in range(2):
                    pemm(ps_c[:, coff + nh * 512:coff + (nh + 1) * 512],
                         cpsw[:, :], qraw[:, nh * 512:(nh + 1) * 512],
                         waits=(w_s if nh == 0 else []),
                         inc=(s_pe if nh == 1 else None))
                swd = s_pe.n
                Ct, St = (CQ, SQ) if wname == "q" else (CK, SK)
                dvtt(scr1[:, :], qraw[:, :], Ct, ALU.mult, waits=[(s_act, aQ)])
                dvtt(scr2[:, :], ps_c[:, coff:coff + 1024], St, ALU.mult,
                     waits=[(s_pe, swd)])
                dstt = qbuf[:, m * 1024:(m + 1) * 1024] if wname == "q" else kst[m % 2]
                dR = dvtt(dstt, scr1[:, :], scr2[:, :], ALU.add, inc=s_dve)
                if wname == "k":
                    kv_st.append(ld(kv_cin.ap()[m * 128:(m + 1) * 128, :], dstt,
                                    waits=[(s_dve, dR)]))
                evac_dve.append(dR)
            else:
                for tj in range(8):
                    slot = TSLOT[tj % 4]
                    w_t = [(s_act, aQ)]
                    if len(vstores) >= 4:
                        w_t.append((s_act, vstores[-4][0]))
                    else:
                        w_t.append((s_dve, P2_done["dve"]))
                    pT = petr(ps_cb[:, slot:slot + 128],
                              qraw[:, tj * 128:(tj + 1) * 128], cid_b[:, :],
                              waits=w_t, inc=s_pe)
                    w_c = [(s_pe, pT)]
                    if len(vstores) >= 2:
                        w_c.append((s_ld, vstores[-2][1]))
                    aP = act(piece[:, (tj % 2) * 128:(tj % 2) * 128 + 128],
                             ps_cb[:, slot:slot + 128], AF.Copy, waits=w_c, inc=s_act)
                    stn = ld(vtm_view[tj * 128:(tj + 1) * 128, m * 128:(m + 1) * 128],
                             piece[:, (tj % 2) * 128:(tj % 2) * 128 + 128],
                             waits=[(s_act, aP)])
                    vstores.append((aP, stn))
                    kv_st.append(stn)
                evac_dve.append(s_dve.n)
        prev_w_pe = s_pe.n
        P2_done = {"act": s_act.n, "dve": s_dve.n}

    KV_LD = s_ld.n

    E("pool", lambda e: e.collective_compute(
        "AllGather", ALU.bypass, replica_groups=GROUPS,
        ins=[kv_cin.ap().opt()], outs=[kv_out.ap().opt()]),
      waits=[(s_ld, KV_LD)], inc=s_cc)
    CC_KV = s_cc.n
    QN_DVE = s_dve.n

    ffn_cast = {}

    def emit_cast(kind, ee):
        src_ = wfc_d if kind == "fc" else wpj_d
        dst_ = wfcb if kind == "fc" else wpjb
        E("pool", lambda en, s_=src_, d_=dst_, i_=ee: en.dma_start(
            out=d_.ap()[i_], in_=s_[i_]),
          waits=[(s_pe, s_pe.n)], inc=s_gd, inck=16)
        ffn_cast[(kind, ee)] = s_gd.n

    expb = [reg_e[:, 0:16384], reg_e[:, 16384:32768]]
    Kb = [reg_k[:, 0:4096], reg_k[:, 4096:8192]]
    Vb = [reg_k[:, 8192:12320], reg_k[:, 12320:16448]]
    VM0 = dvmemset(Vb[0], 1.0, waits=[(s_dve, P2_done["dve"]), (s_act, P2_done["act"])],
                   inc=s_dve)
    VM1 = dvmemset(Vb[1], 1.0, inc=s_dve)

    def kvload(vh, waits):
        par = vh % 2
        w = list(waits)
        for r in range(4):
            ld(Kb[par][:, r * 1024:(r + 1) * 1024],
               kv_out.ap()[r * 4097 + vh * 128:r * 4097 + (vh + 1) * 128, :],
               waits=(w if r == 0 else []))
        for r in range(4):
            src = kv_out.ap()[r * 4097 + 2048:r * 4097 + 4096, :] \
                .rearrange("(t two) c -> t (two c)", two=2)[:, vh * 128:(vh + 1) * 128] \
                .rearrange("(tc p) dv -> p tc dv", p=128)
            dstv = Vb[par][:, r * 1032:(r + 1) * 1032] \
                .rearrange("p (tc c) -> p tc c", c=129)[:, :, 0:128]
            ld(dstv, src)
        return s_ld.n

    kv_ld_n = [0] * 17
    kv_ld_n[0] = kvload(0, [(s_cc, CC_KV), (s_dve, VM0)])
    kv_ld_n[1] = kvload(1, [(s_dve, VM1)])

    phase_pe_end = []
    exp_inc = {}
    evac_acts = []
    last_pair_act = [P2_done["act"], P2_done["act"]]
    scps = [ps_a, ps_b]
    epi_dve_last = QN_DVE
    aRs_par = [0, 0]
    negrow = xn_s
    negrowB = xn_s
    mrow = logit
    NEGM_ACT = [0, 0]
    MAXST_DVE = [QN_DVE, QN_DVE]

    MAXB = ps_c[:, 1536:2048]
    MAXB_T = ps_cb[0:1, 3072:3200]

    def max_stage_steps(vh, waits):
        par = vh % 2
        w0 = list(waits)
        last_red = 0
        last_negm = 0
        for ui in range(16):
            hhp, qt = ui % 2, ui // 2
            for scq in range(8):
                w_u = []
                if ui == 0 and scq == 0:
                    w_u = list(w0)
                if last_red:
                    w_u.append((s_dve, last_red))
                if last_negm and scq == 0:
                    w_u.append((s_act, last_negm))
                pemm(MAXB[:, 0:512],
                     qbuf[hhp * 64:hhp * 64 + 64,
                          vh * 1024 + qt * 128:vh * 1024 + qt * 128 + 128],
                     Kb[par][hhp * 64:hhp * 64 + 64, scq * 512:(scq + 1) * 512],
                     waits=w_u, inc=s_pe)
                pe_u = s_pe.n
                if scq == 0:
                    last_red = dvred(mrow[:, 0:1], MAXB[:, 0:512], ALU.max,
                                     waits=[(s_pe, pe_u)], inc=s_dve)
                else:
                    dvred(mrow[:, 1:2], MAXB[:, 0:512], ALU.max,
                          waits=[(s_pe, pe_u)])
                    last_red = dvtt(mrow[:, 0:1], mrow[:, 0:1], mrow[:, 1:2],
                                    ALU.max, inc=s_dve)
                yield
            dNC = dvts(qraw[:, ui:ui + 1], mrow[:, 0:1], -1.0, None, ALU.mult,
                       inc=s_dve)
            pT = petr(MAXB_T, qraw[:, ui:ui + 1], cid_b[:, :],
                      waits=[(s_dve, dNC)], inc=s_pe)
            qh_, qt4 = qt // 4, qt % 4
            col = par * 1024 + qh_ * 512 + qt4 * 128
            dst_row = negrow[0:1, col:col + 128] if hhp == 0 else \
                negrowB[64:65, col:col + 128]
            last_negm = act(dst_row, MAXB_T, AF.Copy, waits=[(s_pe, pT)], inc=s_act)
            last_red = s_dve.n
            yield
        NEGM_ACT[par] = last_negm
        MAXST_DVE[par] = s_dve.n

    exp_last = [P2_done["act"], P2_done["act"]]
    item_pe_end = []
    for _ in max_stage_steps(0, [(s_ld, kv_ld_n[0]), (s_dve, P2_done["dve"]),
                                 (s_act, P2_done["act"])]):
        pass
    if "negm" in dbg:
        ld(dbg_d["negm"], negrow[0:2, :], waits=[(s_act, NEGM_ACT[0])])
    max_iter = [None]
    for item in range(32):
        vh, qh = item // 2, item % 2
        par = vh % 2
        if qh == 0 and vh + 1 < 16:
            war = item_pe_end[-1] if item_pe_end else 0
            kv_ld_n[vh + 1] = kvload(vh + 1, ([(s_pe, war)] if war else []))
            w_ms = [(s_ld, kv_ld_n[vh + 1])]
            if item_pe_end:
                w_ms.append((s_pe, item_pe_end[-1]))
            if item >= 2:
                w_ms.append((s_pe, item_pe_end[-1]))
            max_iter[0] = max_stage_steps(vh + 1, w_ms)
        exp_war = item_pe_end[item - 2] if item >= 2 else 0
        qoff = vh * 1024 + qh * 512
        for p in range(16):
            for half in range(2):
                sc = p * 2 + half
                for hhp in range(2):
                    w_sc = []
                    if half == 0 and hhp == 0:
                        w_sc = [(s_act, exp_last[0]), (s_act, exp_last[1])]
                        if p == 0:
                            w_sc += [(s_ld, kv_ld_n[vh]), (s_dve, MAXST_DVE[par]),
                                     (s_act, NEGM_ACT[par]), (s_dve, ONES33)]
                            if exp_war:
                                w_sc.append((s_pe, exp_war))
                    pemm(scps[hhp][:, half * 512:(half + 1) * 512],
                         Kb[par][hhp * 64:hhp * 64 + 64, sc * 128:(sc + 1) * 128],
                         qbuf[hhp * 64:hhp * 64 + 64, qoff:qoff + 512],
                         start=True, stop=False, waits=w_sc)
                if half == 1 and max_iter[0] is not None:
                    try:
                        next(max_iter[0])
                    except StopIteration:
                        max_iter[0] = None
            bcol = par * 1024 + qh * 512
            for half in range(2):
                for hhp in range(2):
                    pemm(scps[hhp][:, half * 512:(half + 1) * 512],
                         ones65[0:1, :] if hhp == 0 else ones65[64:65, :],
                         negrow[0:1, bcol:bcol + 512] if hhp == 0 else
                         negrowB[64:65, bcol:bcol + 512],
                         start=False, stop=True,
                         inc=(s_pe if (half == 1 and hhp == 1) else None))
            pe_sc = s_pe.n
            def mstep(k=1):
                if max_iter[0] is not None:
                    for _ in range(k):
                        try:
                            next(max_iter[0])
                        except StopIteration:
                            max_iter[0] = None
                            break
            mstep()
            if p >= 1:
                w_v = [(s_act, exp_inc[(0, p - 1)]), (s_act, exp_inc[(1, p - 1)])]
                if p == 1 and evac_acts:
                    w_v.append((s_act, evac_acts[-1]))
                first = True
                for half in range(2):
                    sc = (p - 1) * 2 + half
                    for qt in range(4):
                        for hhp in range(2):
                            so = NUMOFF[hhp * 4 + qt]
                            pemm(ps_c[:, so:so + 129],
                                 expb[hhp][:, sc * 512 + qt * 128:sc * 512 + qt * 128 + 128],
                                 Vb[par][:, sc * 129:(sc + 1) * 129],
                                 start=(sc == 0), stop=False,
                                 waits=(w_v if first else []))
                            first = False
                        if qt in (1, 3):
                            mstep()
            mstep()
            for hhp in range(2):
                w_e = [(s_pe, pe_sc)]
                exp_inc[(hhp, p)] = act(expb[hhp][:, p * 1024:(p + 1) * 1024],
                                        scps[hhp][:, :], AF.Exp, waits=w_e, inc=s_act)
                exp_last[hhp] = exp_inc[(hhp, p)]
        w_v = [(s_act, exp_inc[(0, 15)]), (s_act, exp_inc[(1, 15)])]
        first = True
        for half in range(2):
            sc = 30 + half
            for qt in range(4):
                for hhp in range(2):
                    so = NUMOFF[hhp * 4 + qt]
                    pemm(ps_c[:, so:so + 129],
                         expb[hhp][:, sc * 512 + qt * 128:sc * 512 + qt * 128 + 128],
                         Vb[par][:, sc * 129:(sc + 1) * 129],
                         start=False, stop=(sc == 31),
                         waits=(w_v if first else []),
                         inc=(s_pe if (half == 1 and qt == 3 and hhp == 1) else None))
                    first = False
        item_pe_end.append(s_pe.n)
        phase_pe_end.append(s_pe.n)
        if qh == 1 and max_iter[0] is not None:
            for _ in max_iter[0]:
                pass
            max_iter[0] = None
        aEv = act(numcp[:, :], ps_c[:, 0:1536], AF.Copy,
                  waits=[(s_pe, item_pe_end[-1]), (s_dve, epi_dve_last)], inc=s_act)
        evac_acts.append(aEv)
        if item == 0 and "numcp" in dbg:
            ld(dbg_d["numcp"], numcp[:, :], waits=[(s_act, aEv)])
        for qt in range(4):
            a0, aB, aC = attbuf[qt % 2]
            oA, oB = NUMOFF[qt], NUMOFF[4 + qt]
            dvrecip(small[:, 20:21], numcp[:, oA + 128:oA + 129], waits=[(s_act, aEv)])
            dvrecip(small[:, 21:22], numcp[:, oB + 128:oB + 129])
            dvtt(small[:, 21:22], small[:, 21:22], lamb[:, 0:1], ALU.mult)
            dvts(a0[:, :], numcp[:, oA:oA + 128], small[:, 20:21], None, ALU.mult,
                 waits=[(s_act, aRs_par[qt % 2])])
            dvts(aB[:, :], numcp[:, oB:oB + 128], small[:, 21:22], None, ALU.mult)
            dSb = dvtt(a0[:, :], a0[:, :], aB[:, :], ALU.subtract, inc=s_dve)
            act(JUNK[:, 0:128], a0[:, :], AF.Square, accum=small[:, 22:23],
                waits=[(s_dve, dSb)])
            act(small[:, 23:24], small[:, 22:23], AF.Ln, scale=1.0 / 128.0,
                bias=small[:, 8:9])
            aRs = act(small[:, 24:25], small[:, 23:24], AF.Exp, scale=-0.5, inc=s_act)
            aRs_par[qt % 2] = aRs
            dvts(aC[:, :], a0[:, :], small[:, 24:25], None, ALU.mult,
                 waits=[(s_act, aRs)])
            col = vh * 1024 + qh * 512 + qt * 128
            epi_dve_last = dvtt(reg_h[:, col:col + 128], aC[:, :], sublb[:, :],
                                ALU.mult, inc=s_dve)
    P3_PE = s_pe.n
    EPI_DVE = epi_dve_last

    tp_acts = []
    for j in range(128):
        slot = TSLOT[j % 4]
        w_t = [(s_dve, EPI_DVE)]
        if j >= 4:
            w_t = [(s_act, tp_acts[j - 4])]
        pT = petr(ps_cb[:, slot:slot + 128], reg_h[:, j * 128:(j + 1) * 128],
                  cid_b[:, :], waits=w_t, inc=s_pe)
        tp_acts.append(act(reg_h[:, j * 128:(j + 1) * 128], ps_cb[:, slot:slot + 128],
                           AF.Copy, waits=[(s_pe, pT)], inc=s_act))
    ATTNT_ACT = tp_acts[-1]
    if "attnt" in dbg:
        ld(dbg_d["attnt"], reg_h[:, :], waits=[(s_act, ATTNT_ACT)])

    for k in range(16):
        ldt(reg_e[:, k * 2048:(k + 1) * 2048], wo_bf.ap()[:, k * 128:(k + 1) * 128],
            waits=([(s_gd, GD_WO), (s_pe, P3_PE)] if k == 0 else []))
    WOT_LD = s_ld.n
    h2 = reg_q
    wo_slots = [(ps_a, 0), (ps_a, 512), (ps_b, 0), (ps_b, 512)]
    wo_evac = [0] * 4
    x1_hist = []
    h2_hist = []
    x1_st = []
    for tt in range(8):
        if tt % 2 == 0:
            emit_cast("fc" if tt % 4 == 0 else "pj", tt // 4)
        xst = reg_kf[:, (tt % 2) * 2048:(tt % 2) * 2048 + 2048]
        w = [(s_pe, P3_PE)]
        if tt >= 2:
            w.append((s_dve, x1_hist[tt - 2]))
        xl = ld(xst, x_d[tt * 128:(tt + 1) * 128, :], waits=w)
        x1t = reg_kf[:, 4096:6144]
        for m in range(16):
            tgt, off = wo_slots[m % 4]
            w_m = [(s_ld, WOT_LD), (s_act, ATTNT_ACT)]
            if wo_evac[m % 4]:
                w_m.append((s_dve, wo_evac[m % 4]))
            if tt > 0 and m == 0:
                w_m.append((s_dve, h2_hist[-1]))
            for k in range(16):
                pemm(tgt[:, off:off + 128],
                     reg_e[:, k * 2048 + m * 128:k * 2048 + m * 128 + 128],
                     reg_h[:, k * 1024 + tt * 128:k * 1024 + tt * 128 + 128],
                     start=(k == 0), stop=(k == 15),
                     waits=(w_m if k == 0 else []),
                     inc=(s_pe if k == 15 else None))
            mmd = s_pe.n
            dG = dvts(piece[:, (m % 2) * 128:(m % 2) * 128 + 128], tgt[:, off:off + 128],
                      camod[:, 64 + m:65 + m], None, ALU.mult,
                      waits=[(s_pe, mmd), (s_ld, xl)], inc=s_dve)
            slot = TSLOT[m % 4]
            pT = petr(ps_cb[:, slot:slot + 128], piece[:, (m % 2) * 128:(m % 2) * 128 + 128],
                      cid_b[:, :], waits=[(s_dve, dG)], inc=s_pe)
            w_x1 = [(s_pe, pT)]
            if m == 0 and x1_st:
                w_x1.append((s_ld, x1_st[-1]))
            wo_evac[m % 4] = dvtt(x1t[:, m * 128:(m + 1) * 128],
                                  xst[:, m * 128:(m + 1) * 128],
                                  ps_cb[:, slot:slot + 128], ALU.add,
                                  waits=w_x1, inc=s_dve)
        dX1 = s_dve.n
        P_ = (tt % 2) * 16
        act(xn_s[:, :], x1t[:, :], AF.Square, accum=small[:, P_ + 3:P_ + 4],
            waits=[(s_dve, dX1)])
        act(xn_s[:, :], x1t[:, :], AF.Copy, accum=small[:, P_ + 2:P_ + 3])
        aS2 = act(small[:, 10:11], small[:, 8:9], AF.Copy, inc=s_act)
        dvts(small[:, P_ + 0:P_ + 1], small[:, P_ + 2:P_ + 3], 1.0 / D, None, ALU.mult,
             waits=[(s_act, aS2)])
        dvts(small[:, P_ + 4:P_ + 5], small[:, P_ + 3:P_ + 4], 1.0 / D, None, ALU.mult)
        dvtt(small[:, P_ + 5:P_ + 6], small[:, P_ + 0:P_ + 1], small[:, P_ + 0:P_ + 1],
             ALU.mult)
        dV2 = dvtt(small[:, P_ + 6:P_ + 7], small[:, P_ + 4:P_ + 5],
                   small[:, P_ + 5:P_ + 6], ALU.subtract, inc=s_dve)
        act(small[:, P_ + 5:P_ + 6], small[:, P_ + 6:P_ + 7], AF.Ln,
            bias=small[:, 8:9], waits=[(s_dve, dV2)])
        aR2 = act(small[:, P_ + 1:P_ + 2], small[:, P_ + 5:P_ + 6], AF.Exp, scale=-0.5,
                  inc=s_act)
        dXN2 = dvts(xn_s[:, :], x1t[:, :], small[:, P_ + 0:P_ + 1],
                    small[:, P_ + 1:P_ + 2], ALU.subtract, ALU.mult,
                    waits=[(s_act, aR2)], inc=s_dve)
        ev2 = []
        for k in range(16):
            slot = TSLOT[k % 4]
            w2 = [(s_dve, dXN2)]
            if k >= 4:
                w2 = [(s_dve, ev2[k - 4])]
            pT = petr(ps_cb[:, slot:slot + 128], xn_s[:, k * 128:(k + 1) * 128],
                      cid_b[:, :], waits=w2, inc=s_pe)
            ev2.append(dvts(h2[:, k * 1024 + tt * 128:k * 1024 + tt * 128 + 128],
                            ps_cb[:, slot:slot + 128],
                            camod[:, 32 + k:33 + k], camod[:, 48 + k:49 + k],
                            ALU.mult, ALU.add, waits=[(s_pe, pT)], inc=s_dve))
        h2_hist.append(s_dve.n)
        x1_hist.append(s_dve.n)
        x1_st.append(ld(x1_dram.ap()[tt * 128:(tt + 1) * 128, :], x1t,
                        waits=[(s_dve, x1_hist[-1])]))
    H2_DVE = s_dve.n
    X1_LD = s_ld.n
    WOUT_PE = s_pe.n
    if "h2" in dbg:
        ld(dbg_d["h2"], h2[:, :], waits=[(s_dve, H2_DVE)])

    rt_done = []
    for tt in range(8):
        w_r = [(s_dve, H2_DVE)]
        if tt >= 2:
            w_r.append((s_dve, rt_done[tt - 2]))
        for k in range(16):
            pemm(ps_c[:, (tt % 2) * 512:(tt % 2) * 512 + 8],
                 h2[:, k * 1024 + tt * 128:k * 1024 + tt * 128 + 128],
                 wrtb[:, k * 8:(k + 1) * 8],
                 start=(k == 0), stop=(k == 15),
                 waits=(w_r if k == 0 else []),
                 inc=(s_pe if k == 15 else None))
        rmm = s_pe.n
        w_lg = [(s_pe, rmm)]
        if rt_done:
            w_lg.append((s_dve, rt_done[-1]))
        aLg = act(logit[:, 0:8], ps_c[:, (tt % 2) * 512:(tt % 2) * 512 + 8], AF.Copy,
                  waits=w_lg, inc=s_act)
        dvred(small[:, 25:26], logit[:, 0:8], ALU.max, waits=[(s_act, aLg)])
        dvts(logit[:, 8:16], logit[:, 0:8], small[:, 25:26], None, ALU.is_equal)
        dvts(logit[:, 16:24], logit[:, 8:16], -1e30, None, ALU.mult)
        dvtt(logit[:, 8:16], logit[:, 0:8], logit[:, 16:24], ALU.add)
        dvred(small[:, 26:27], logit[:, 8:16], ALU.max)
        dM2 = dvts(small[:, 27:28], small[:, 25:26], -1.0, None, ALU.mult, inc=s_dve)
        act(logit[:, 16:24], logit[:, 0:8], AF.Exp, bias=small[:, 27:28],
            waits=[(s_dve, dM2)])
        aE2 = act(small[:, 28:29], small[:, 26:27], AF.Exp, bias=small[:, 27:28],
                  inc=s_act)
        dvts(small[:, 30:31], small[:, 28:29], 1.0, None, ALU.add, waits=[(s_act, aE2)])
        dvrecip(small[:, 29:30], small[:, 30:31])
        dvts(logit[:, 24:32], logit[:, 0:8], small[:, 26:27], None, ALU.is_ge)
        dvtt(logit[:, 24:32], logit[:, 24:32], logit[:, 16:24], ALU.mult)
        rt_done.append(dvts(wtok[:, tt * 8:(tt + 1) * 8], logit[:, 24:32],
                            small[:, 29:30], None, ALU.mult, inc=s_dve))
    RT_DVE = rt_done[-1]
    WTR_ACT = s_act.n
    if "wtok" in dbg:
        ld(dbg_d["wtok"], wtok[:, :], waits=[(s_dve, RT_DVE)])

    ACC0 = dvmemset(reg_h[:, :], 0.0, waits=[(s_pe, WOUT_PE)], inc=s_dve)
    h2mid = reg_k[:, 0:16384]
    fcslot = [reg_e[:, 0:8192], reg_e[:, 8192:16384]]
    pjslot = [reg_e[:, 16384:24576], reg_e[:, 24576:32768]]
    RELU = strm[:, 8192:9216]
    fc_pe = []
    fc_sq = []
    pj_pe = []
    acc_dve = []
    wtb_act = 0
    for e in range(8):
        if e < 6:
            emit_cast("fc", e + 2)
            emit_cast("pj", e + 2)
        wrow6 = numcp[0:1, 1024:1152]
        wtb_act = 0
        for tt in range(8):
            w_w = [(s_dve, RT_DVE)]
            if acc_dve:
                w_w.append((s_dve, acc_dve[-1]))
            if wtb_act:
                w_w.append((s_act, wtb_act))
            pT = petr(ps_c[0:1, 1536 + (tt % 2) * 128:1536 + (tt % 2) * 128 + 128],
                      wtok[:, tt * 8 + e:tt * 8 + e + 1], cid_f[:, :],
                      waits=w_w, inc=s_pe)
            aRow = act(wrow6, ps_c[0:1, 1536 + (tt % 2) * 128:1536 + (tt % 2) * 128 + 128],
                       AF.Copy, waits=[(s_pe, pT)], inc=s_act)
            peB = pemm(ps_c[:, (tt % 2) * 512:(tt % 2) * 512 + 128], co1f[:, :], wrow6,
                       waits=[(s_act, aRow)], inc=s_pe)
            wtb_act = act(wtb[:, tt * 128:(tt + 1) * 128],
                          ps_c[:, (tt % 2) * 512:(tt % 2) * 512 + 128],
                          AF.Copy, waits=[(s_pe, peB)], inc=s_act)
        for g in range(4):
            slot = g % 2
            w_d = [(s_pe, WOUT_PE), (s_gd, ffn_cast[("fc", e)])]
            if len(fc_pe) >= 2:
                w_d = [(s_pe, fc_pe[-2]), (s_gd, ffn_cast[("fc", e)])]
            ldn = ld(fcslot[slot].rearrange("p (k d) -> p k d", d=512),
                     wfcb.ap()[e, :, g * 512:(g + 1) * 512]
                     .rearrange("(k p) d -> p k d", p=128),
                     waits=w_d)
            for dmi in range(4):
                dm = g * 4 + dmi
                tgt = ps_a if dm % 2 == 0 else ps_b
                w_f = [(s_ld, ldn), (s_dve, ACC0)]
                if len(fc_sq) >= 2:
                    w_f.append((s_act, fc_sq[-2]))
                if e > 0 and dm == 0 and pj_pe:
                    w_f.append((s_pe, pj_pe[-1]))
                for nh in range(2):
                    for k in range(16):
                        pemm(tgt[:, nh * 512:(nh + 1) * 512],
                             fcslot[slot][:, k * 512 + dmi * 128:k * 512 + dmi * 128 + 128],
                             h2[:, k * 1024 + nh * 512:k * 1024 + nh * 512 + 512],
                             start=(k == 0), stop=(k == 15),
                             waits=(w_f if (nh == 0 and k == 0) else []),
                             inc=(s_pe if (nh == 1 and k == 15) else None))
                fc_pe.append(s_pe.n)
                w_rel = [(s_pe, fc_pe[-1])]
                if fc_sq:
                    w_rel.append((s_act, fc_sq[-1]))
                dRel = dvts(RELU, tgt[:, :], 0.0, None, ALU.max, waits=w_rel, inc=s_dve)
                w_sq = [(s_dve, dRel)]
                if e > 0 and pj_pe:
                    w_sq.append((s_pe, pj_pe[-1]))
                fc_sq.append(act(h2mid[:, dm * 1024:(dm + 1) * 1024], RELU, AF.Square,
                                 waits=w_sq, inc=s_act))
        FC_ACT = fc_sq[-1]
        for g in range(4):
            slot = g % 2
            w_d = [(s_pe, WOUT_PE), (s_gd, ffn_cast[("pj", e)])]
            if len(pj_pe) >= 2:
                w_d = [(s_pe, pj_pe[-2]), (s_gd, ffn_cast[("pj", e)])]
            ldn = ld(pjslot[slot].rearrange("p (k d) -> p k d", d=512),
                     wpjb.ap()[e, :, g * 512:(g + 1) * 512]
                     .rearrange("(k p) d -> p k d", p=128),
                     waits=w_d)
            for cmi in range(4):
                cm = g * 4 + cmi
                tps = ps_c[:, 0:1024] if cm % 2 == 0 else ps_c[:, 1024:2048]
                w_p = [(s_ld, ldn), (s_act, FC_ACT), (s_act, wtb_act)]
                if len(acc_dve) >= 2:
                    w_p.append((s_dve, acc_dve[-2]))
                for nh in range(2):
                    for dm in range(16):
                        pemm(tps[:, nh * 512:(nh + 1) * 512],
                             pjslot[slot][:, dm * 512 + cmi * 128:dm * 512 + cmi * 128 + 128],
                             h2mid[:, dm * 1024 + nh * 512:dm * 1024 + nh * 512 + 512],
                             start=(dm == 0), stop=(dm == 15),
                             waits=(w_p if (nh == 0 and dm == 0) else []),
                             inc=(s_pe if (nh == 1 and dm == 15) else None))
                pj_pe.append(s_pe.n)
                dT = dvtt(scr1[:, :], tps[:, :], wtb[:, :], ALU.mult,
                          waits=[(s_pe, pj_pe[-1])])
                acc_dve.append(dvtt(reg_h[:, cm * 1024:(cm + 1) * 1024],
                                    reg_h[:, cm * 1024:(cm + 1) * 1024], scr1[:, :],
                                    ALU.add, inc=s_dve))
    P6_DVE = acc_dve[-1]
    P6_PE = s_pe.n
    if "acc" in dbg:
        ld(dbg_d["acc"], reg_h[:, :], waits=[(s_dve, P6_DVE)])

    GB_DVE = P6_DVE
    out_dve = []
    pg_hist = []
    for tt in range(8):
        xst = reg_kf[:, 0:2048]
        yst = reg_kf[:, 2048:4096]
        w = [(s_pe, P6_PE)]
        if tt > 0:
            w.append((s_out, tt * 16))
        xl = ld(xst, x1_dram.ap()[tt * 128:(tt + 1) * 128, :], waits=w)
        for cm in range(16):
            slot = TSLOT[cm % 4]
            w_g = [(s_dve, GB_DVE), (s_pe, P6_PE)]
            if len(out_dve) >= 2:
                w_g.append((s_pe, pg_hist[-2]))
            dg = dvts(piece[:, (cm % 2) * 128:(cm % 2) * 128 + 128],
                      reg_h[:, cm * 1024 + tt * 128:cm * 1024 + tt * 128 + 128],
                      camod[:, 80 + cm:81 + cm], None, ALU.mult,
                      waits=w_g, inc=s_dve)
            w_t = [(s_dve, dg)]
            if len(out_dve) >= 4:
                w_t.append((s_dve, out_dve[-4]))
            pT = petr(ps_cb[:, slot:slot + 128],
                      piece[:, (cm % 2) * 128:(cm % 2) * 128 + 128],
                      cid_b[:, :], waits=w_t, inc=s_pe)
            pg_hist.append(pT)
            out_dve.append(dvtt(yst[:, cm * 128:(cm + 1) * 128],
                                xst[:, cm * 128:(cm + 1) * 128],
                                ps_cb[:, slot:slot + 128], ALU.add,
                                waits=[(s_pe, pT), (s_ld, xl)], inc=s_dve))
        E("syncq", lambda e, t=tt, y=yst: e.dma_start(
            out=y_d[t * 128:(t + 1) * 128, :], in_=y),
          waits=[(s_dve, out_dve[-1])], inc=s_out, inck=16)

    if "modv" in dbg:
        ld(dbg_d["modv"], modv[:, :], waits=[(s_act, aMV)])
    if "camod" in dbg:
        ld(dbg_d["camod"], camod[:, :], waits=[(s_dve, DVE_MOD)])
    if "lamb" in dbg:
        ld(dbg_d["lamb"], lamb[:, :], waits=[(s_dve, DVE_MOD)])
    if False and "h1" in dbg:
        ld(dbg_d["h1"], h1[:, :], waits=[(s_dve, P1_DVE)])
    if "qrot" in dbg:
        ld(dbg_d["qrot"], qbuf[:, :], waits=[(s_dve, P2_done["dve"])])
    if "kcin" in dbg:
        ld(dbg_d["kcin"], kv_cin.ap()[:, :], waits=[(s_ld, KV_LD)])
    if False and "attnt" in dbg:
        ld(dbg_d["attnt"], reg_h[:, :], waits=[(s_act, ATTNT_ACT)])
    if "x1" in dbg:
        ld(dbg_d["x1"], x1_dram.ap()[:, :], waits=[(s_ld, X1_LD)])
    if False and "h2" in dbg:
        ld(dbg_d["h2"], h2[:, :], waits=[(s_dve, H2_DVE)])
    if False and "wtok" in dbg:
        ld(dbg_d["wtok"], wtok[:, :], waits=[(s_dve, RT_DVE)])
    if False and "acc" in dbg:
        ld(dbg_d["acc"], reg_h[:, :], waits=[(s_dve, P6_DVE)])

    E("syncq", lambda e: e.nop(), waits=[(s_out, s_out.n), (s_ld, s_ld.n)])

    blk = stk.enter_context(nc.Block())

    @blk.tensor
    def _(t):
        for f in sch.ops["pe"]:
            f(t)

    @blk.scalar
    def _(s):
        for f in sch.ops["act"]:
            f(s)

    @blk.vector
    def _(v):
        for f in sch.ops["dve"]:
            f(v)

    @blk.gpsimd
    def _(g):
        for f in sch.ops["pool"]:
            f(g)

    @blk.sync
    def _(sy):
        for f in sch.ops["syncq"]:
            f(sy)

    stk.close()
    nc._sched_trace = sch
    return nc, dbg_d


_BUILT = {}


def _get_program(dbg=()):
    key = tuple(sorted(dbg))
    if key not in _BUILT:
        _BUILT[key] = build_program(dbg)
    return _BUILT[key]


def make_inmaps(inputs):
    lams = np.concatenate([
        np.asarray(inputs["L0_attn_lambda_q1"]), np.asarray(inputs["L0_attn_lambda_k1"]),
        np.asarray(inputs["L0_attn_lambda_q2"]), np.asarray(inputs["L0_attn_lambda_k2"])]
    ).reshape(1, -1).astype(np.float32)
    x = np.asarray(inputs["L0_attn_x"])
    wmod = np.asarray(inputs["L0_wmod"])
    c = np.asarray(inputs["L0_c"])
    in_maps = []
    for core in range(NCORES):
        b, tq = core // 4, core % 4
        m = dict(_host_consts(core))
        m["x_sl"] = np.ascontiguousarray(x[b, tq * TC:(tq + 1) * TC, :])
        m["wq"] = np.asarray(inputs["L0_attn_wq"])
        m["wk"] = np.asarray(inputs["L0_attn_wk"])
        m["wv"] = np.asarray(inputs["L0_attn_wv"])
        m["wout"] = np.asarray(inputs["L0_attn_wout"])
        m["wfc"] = np.asarray(inputs["L0_ffn_w_fc"])
        m["wproj"] = np.asarray(inputs["L0_ffn_w_proj"])
        m["wrouter"] = np.asarray(inputs["L0_ffn_w_router"])
        m["wmod_sl"] = np.ascontiguousarray(wmod[:, tq * MODW:(tq + 1) * MODW])
        m["c2"] = np.ascontiguousarray(c[[b, 1 - b], :])
        m["lams"] = lams
        m["subln"] = np.asarray(inputs["L0_attn_subln_weight"]).reshape(1, -1).astype(np.float32)
        in_maps.append(m)
    return in_maps


def kernel(**inputs):
    nc, _ = _get_program()
    in_maps = make_inmaps(inputs)
    res = run_bass_kernel_spmd(nc, in_maps, core_ids=list(range(NCORES)))
    out = np.empty((B, T, D), np.float32)
    for core in range(NCORES):
        b, tq = core // 4, core % 4
        out[b, tq * TC:(tq + 1) * TC, :] = res.results[core]["y"]
    return out


# version 13
# speedup vs baseline: 1.0898x; 1.0596x over previous
import numpy as np
import ml_dtypes
import concourse.bass as bass
import concourse.mybir as mybir
from concourse.bass_utils import run_bass_kernel_spmd

BF = ml_dtypes.bfloat16
F32 = mybir.dt.float32
BF16 = mybir.dt.bfloat16
AF = mybir.ActivationFunctionType
ALU = mybir.AluOpType
AX = mybir.AxisListType

B, T, D = 2, 4096, 2048
TC = 1024
NCORES = 8
HALF = 64
SCALE = float(128.0 ** 0.5)
LAMBDA_INIT = 0.8 - 0.6 * float(np.exp(-0.3 * 12))
EPS = 1e-5
MODW = 3072
GROUPS = [[0, 1, 2, 3], [4, 5, 6, 7]]
ROPE0 = 8192
NUMOFF = [(s // 3) * 512 + (s % 3) * 129 for s in range(8)]
TSLOT = [0, 1024, 2048, 3072]


def _rope_tables():
    inv = 1.0 / 10000.0 ** (np.arange(0, HALF, 2)[: HALF // 2].astype(np.float64) / HALF)
    ang = np.outer(np.arange(T), inv)
    return np.cos(ang).astype(np.float32), np.sin(ang).astype(np.float32)


def _host_consts(core):
    tq = core % 4
    cos, sin = _rope_tables()
    sl = slice(tq * TC, (tq + 1) * TC)
    c, s = cos[sl].T, sin[sl].T
    idx = (np.arange(128) % HALF) // 2
    Crow = c[idx]
    Srow = s[idx] * np.where(np.arange(128) % 2 == 0, -1.0, 1.0)[:, None]
    psw = np.zeros((128, 128), np.float32)
    i = np.arange(0, 128, 2)
    psw[i, i + 1] = 1.0
    psw[i + 1, i] = 1.0
    return {
        "ropecq": np.ascontiguousarray((Crow * SCALE).astype(BF)),
        "ropesq": np.ascontiguousarray((Srow * SCALE).astype(BF)),
        "ropeck": np.ascontiguousarray(Crow.astype(BF)),
        "ropesk": np.ascontiguousarray(Srow.astype(BF)),
        "pswap": psw.astype(BF),
        "identb": np.eye(128, dtype=np.float32).astype(BF),
        "identf": np.eye(128, dtype=np.float32),
        "ones128b": np.ones((128, 1), np.float32).astype(BF),
        "ones1f": np.ones((1, 128), np.float32),
        "ones1b": np.ones((1, 128), np.float32).astype(BF),
    }


class _S:
    def __init__(self, handle, name):
        self.h = handle
        self.name = name
        self.n = 0


class _Sched:
    QUEUES = ("pe", "act", "dve", "pool", "syncq")

    def __init__(self):
        self.ops = {q: [] for q in self.QUEUES}
        self.guar = {q: {} for q in self.QUEUES}
        self.trace = []
        self.sdve = None
        self.dve_pend = 0

    def emit(self, q, builder, waits=(), inc=None, inck=1, dvw=None):
        if q == "dve" and self.dve_pend:
            waits = list(waits) + [(self.sdve, self.dve_pend)]
            self.dve_pend = 0
        g = self.guar[q]
        baked = []
        rec_w = []
        for (s, v) in waits:
            rec_w.append((s.name, v))
            if v > 0 and g.get(s.name, 0) < v:
                baked.append((s.h, v))
                g[s.name] = v
        if q == "dve" and dvw is not None and dvw < 64 and inc is None:
            inc = self.sdve
        self.trace.append((q, tuple(rec_w), (inc.name if inc else None), inck,
                           getattr(builder, "__name__", "op")))
        if inc is not None:
            inc.n += inck
        if q == "dve" and dvw is not None and dvw < 64:
            self.dve_pend = inc.n
        def f(eng, baked=baked, builder=builder, ih=(inc.h if inc else None), ik=inck):
            for h, v in baked:
                eng.wait_ge(h, v)
            ins = builder(eng)
            if ih is not None:
                ins.then_inc(ih, ik)
        self.ops[q].append(f)


def build_program(dbg=()):
    nc = bass.Bass("TRN2", target_bir_lowering=False)
    dbg = set(dbg)

    dti = lambda name, shape, dty=F32: nc.dram_tensor(name, shape, dty, kind="ExternalInput").ap()
    x_d = dti("x_sl", [TC, D])
    wq_d = dti("wq", [D, D]); wk_d = dti("wk", [D, D]); wv_d = dti("wv", [D, D]); wo_d = dti("wout", [D, D])
    wfc_d = dti("wfc", [8, D, D]); wpj_d = dti("wproj", [8, D, D])
    wrt_d = dti("wrouter", [D, 8])
    wmod_d = dti("wmod_sl", [D, MODW])
    c2_d = dti("c2", [2, D])
    lam_d = dti("lams", [1, 4 * HALF])
    sub_d = dti("subln", [1, 128])
    cq_d = dti("ropecq", [128, TC], BF16); sq_d = dti("ropesq", [128, TC], BF16)
    ck_d = dti("ropeck", [128, TC], BF16); sk_d = dti("ropesk", [128, TC], BF16)
    psw_d = dti("pswap", [128, 128], BF16)
    idb_d = dti("identb", [128, 128], BF16)
    idf_d = dti("identf", [128, 128], F32)
    o128_d = dti("ones128b", [128, 1], BF16)
    o1f_d = dti("ones1f", [1, 128], F32)
    o1b_d = dti("ones1b", [1, 128], BF16)

    y_d = nc.dram_tensor("y", [TC, D], F32, kind="ExternalOutput").ap()
    DBG_MENU = {
        "h1": ([128, 16384], BF16), "qrot": ([128, 16384], BF16),
        "kcin": ([4097, TC], BF16), "modv": ([128, 96], F32),
        "camod": ([128, 96], F32), "negcb": ([128, 32], F32),
        "lamb": ([128, 1], F32), "attnt": ([128, 16384], BF16),
        "x1": ([TC, D], F32), "h2": ([128, 16384], BF16),
        "wtok": ([128, 64], F32), "acc": ([128, 16384], BF16),
        "nstat": ([1, 128], F32), "ltile": ([128, 256], BF16),
        "negm": ([2, 2048], BF16), "mrow": ([128, 32], F32),
        "numcp": ([128, 1536], F32),
        "lraw": ([1, 256], BF16), "lsmall": ([128, 32], F32),
        "p1small": ([128, 32], F32), "p1xn": ([128, 2048], BF16),
    }
    dbg_d = {n: nc.dram_tensor("dbg_" + n, DBG_MENU[n][0], DBG_MENU[n][1],
                               kind="ExternalOutput").ap() for n in dbg}

    wq_bf = nc.dram_tensor("wq_bf", [D, D], BF16)
    wk_bf = nc.dram_tensor("wk_bf", [D, D], BF16)
    wv_bf = nc.dram_tensor("wv_bf", [D, D], BF16)
    wo_bf = nc.dram_tensor("wo_bf", [D, D], BF16)
    negm_dram = nc.dram_tensor("negm_dram", [1, 2048], BF16)
    wmod_bf = nc.dram_tensor("wmod_bf", [D, MODW], BF16)
    kv_cin = nc.dram_tensor("kv_cin", [4097, TC], BF16)
    kv_out = nc.dram_tensor("kv_out", [4 * 4097, TC], BF16)
    mod_cin = nc.dram_tensor("mod_cin", [1, MODW], F32)
    mod_out = nc.dram_tensor("mod_out", [4, MODW], F32)
    x1_dram = nc.dram_tensor("x1_dram", [TC, D], F32)

    from contextlib import ExitStack
    stk = ExitStack()
    sb = lambda name, shape, dty: stk.enter_context(nc.sbuf_tensor(name, shape, dty))
    psb = lambda name, shape, dty: stk.enter_context(nc.psum_tensor(name, shape, dty))

    reg_h = sb("reg_h", [128, 16384], BF16)
    reg_q = sb("reg_q", [128, 16384], BF16)
    reg_e = sb("reg_e", [128, 32768], BF16)
    reg_k = sb("reg_k", [128, 16448], BF16)
    strm = sb("strm", [128, 9216], BF16)
    scr1 = sb("scr1", [128, 1024], F32)
    scr2 = sb("scr2", [128, 1024], F32)
    numcp = sb("numcp", [128, 1536], F32)
    xn_s = sb("xn_s", [128, 2048], BF16)
    qraw = sb("qraw", [128, 1024], BF16)
    small = sb("small", [128, 32], F32)
    modv = sb("modv", [128, 96], F32)
    camod = sb("camod", [128, 96], F32)
    negcb = sb("negcb", [128, 32], F32)
    lamb = sb("lamb", [128, 1], F32)
    sublb = sb("sublb", [128, 128], F32)
    wrtb = sb("wrtb", [128, 128], BF16)
    wtok = sb("wtok", [128, 64], F32)
    logit = sb("logit", [128, 32], F32)
    nstat = sb("nstat", [1, 128], F32)
    k2loc = sb("k2loc", [1, 32], F32)
    k2bf = sb("k2bf", [1, 128], BF16)
    modm = sb("modm", [96, 128], F32)
    cid_b = sb("cid_b", [128, 128], BF16)
    cid_f = sb("cid_f", [128, 128], F32)
    cpsw = sb("cpsw", [128, 128], BF16)
    co128 = sb("co128", [128, 1], BF16)
    co1f = sb("co1f", [1, 128], F32)
    co1b = sb("co1b", [1, 128], BF16)
    silct = sb("silct", [128, 32], BF16)
    piece = sb("piece", [128, 256], BF16)
    ones65 = sb("ones65", [65, 128], BF16)
    attbuf = [(scr2[:, 0:128], scr2[:, 128:256], scr2[:, 256:384]),
              (scr2[:, 384:512], scr2[:, 512:640], scr2[:, 640:768])]
    wtb = scr2[:, 0:1024]
    csilb = reg_e[0:2, 4096:6144]
    ltile_t = sb("ltile", [128, 256], BF16)
    ltile = ltile_t[:, :]
    lraw_t = sb("lraw", [1, 256], BF16)
    lraw = lraw_t[:, :]
    sraw_t = sb("sraw", [1, 128], F32)
    sraw = sraw_t[:, :]

    ps_a = psb("ps_a", [128, 1024], F32)
    ps_b = psb("ps_b", [128, 1024], F32)
    ps_c = psb("ps_c", [128, 2048], F32)
    ps_cb = ps_c.bitcast(BF16)
    ps_ab = ps_a.bitcast(BF16)

    sem = lambda name: _S(stk.enter_context(nc.semaphore(name)), name)
    s_ld = sem("s_ld"); s_gd = sem("s_gd")
    s_pe = sem("s_pe"); s_act = sem("s_act"); s_dve = sem("s_dve")
    s_cc = sem("s_cc"); s_out = sem("s_out")

    sch = _Sched()
    sch.sdve = s_dve
    E = sch.emit

    def ld(dst_ap, src_ap, q="syncq", waits=()):
        s = s_ld if q == "syncq" else s_gd
        E(q, lambda e, d=dst_ap, sr=src_ap: e.dma_start(out=d, in_=sr),
          waits=waits, inc=s, inck=16)
        return s.n

    def ldt(dst_ap, src_ap, waits=()):
        E("syncq", lambda e, d=dst_ap, sr=src_ap: e.dma_start(out=d, in_=sr, transpose=True),
          waits=waits, inc=s_ld, inck=16)
        return s_ld.n

    def pemm(out_ap, l_ap, r_ap, start=True, stop=True, waits=(), inc=None):
        E("pe", lambda e, o=out_ap, l=l_ap, r=r_ap, st=start, sp=stop:
          e.matmul(o, l, r, start=st, stop=sp), waits=waits, inc=inc)
        return s_pe.n

    def petr(out_ap, in_ap, ident_ap, waits=(), inc=None):
        E("pe", lambda e, o=out_ap, i=in_ap, d=ident_ap: e.transpose(o, i, d),
          waits=waits, inc=inc)
        return s_pe.n

    def act(out_ap, in_ap, func, bias=None, scale=1.0, accum=None, waits=(), inc=None):
        def f(e, o=out_ap, i=in_ap, fn=func, b=bias, s=scale, a=accum):
            kw = dict(func=fn, scale=s)
            if b is not None:
                kw["bias"] = b
            if a is not None:
                kw["accum_out"] = a
            return e.activation(out=o, in_=i, **kw)
        E("act", f, waits=waits, inc=inc)
        return s_act.n

    def _w(ap):
        try:
            sh = list(ap.shape)
        except Exception:
            return None
        n = 1
        for d in sh[1:]:
            n *= int(d)
        return n

    def dvts(out_ap, in_ap, sc1, sc2, op0, op1=None, waits=(), inc=None):
        def f(e, o=out_ap, i=in_ap, a=sc1, b=sc2, p0=op0, p1=op1):
            if p1 is None:
                return e.tensor_scalar(o, i, a, None, p0)
            return e.tensor_scalar(o, i, a, b, p0, p1)
        E("dve", f, waits=waits, inc=inc, dvw=_w(out_ap))
        return s_dve.n

    def dvtt(out_ap, a_ap, b_ap, op, waits=(), inc=None):
        E("dve", lambda e, o=out_ap, a=a_ap, b=b_ap, p=op: e.tensor_tensor(o, a, b, p),
          waits=waits, inc=inc, dvw=_w(out_ap))
        return s_dve.n

    def dvred(out_ap, in_ap, op, waits=(), inc=None):
        E("dve", lambda e, o=out_ap, i=in_ap, p=op: e.tensor_reduce(o, i, AX.X, p),
          waits=waits, inc=inc, dvw=_w(out_ap))
        return s_dve.n

    def dvcp(out_ap, in_ap, waits=(), inc=None):
        E("dve", lambda e, o=out_ap, i=in_ap: e.tensor_copy(o, i), waits=waits, inc=inc,
          dvw=_w(out_ap))
        return s_dve.n

    def dvrecip(out_ap, in_ap, waits=(), inc=None):
        E("dve", lambda e, o=out_ap, i=in_ap: e.reciprocal(o, i), waits=waits, inc=inc,
          dvw=_w(out_ap))
        return s_dve.n

    def dvmemset(ap, val, waits=(), inc=None):
        E("dve", lambda e, a=ap, v=val: e.memset(a, v), waits=waits, inc=inc,
          dvw=_w(ap))
        return s_dve.n

    ld(cid_b[:, :], idb_d)
    ld(cid_f[:, :], idf_d)
    ld(cpsw[:, :], psw_d)
    ld(co128[:, :], o128_d)
    ld(co1f[:, :], o1f_d)
    ld(co1b[:, :], o1b_d)
    ld(reg_k[:, ROPE0:ROPE0 + 1024], cq_d)
    ld(reg_k[:, ROPE0 + 1024:ROPE0 + 2048], sq_d)
    ld(reg_k[:, ROPE0 + 2048:ROPE0 + 3072], ck_d)
    ld(reg_k[:, ROPE0 + 3072:ROPE0 + 4096], sk_d)
    ld(lraw, lam_d, q="pool")
    ld(wrtb[:, :].rearrange("p (k e) -> p k e", e=8),
       wrt_d.rearrange("(k p) e -> p k e", p=128), q="pool")
    ld(sraw, sub_d)
    ld(csilb, c2_d, q="pool")
    E("pool", lambda e: e.dma_start(out=wmod_bf.ap()[:, :], in_=wmod_d),
      inc=s_gd, inck=16)
    GD_WMOD = s_gd.n
    P0_LD = s_ld.n
    P0_GD = s_gd.n
    dvmemset(small[:, 8:9], EPS)
    ONES33 = dvmemset(ones65[:, :], 1.0, inc=s_dve)

    peL = pemm(ps_c[:, 0:256], co1b[:, :], lraw,
               waits=[(s_ld, P0_LD), (s_gd, P0_GD)], inc=s_pe)
    aL = act(ltile, ps_c[:, 0:256], AF.Copy, waits=[(s_pe, peL)], inc=s_act)
    dvtt(scr1[:, 0:64], ltile[:, 0:64], ltile[:, 64:128], ALU.mult, waits=[(s_act, aL)])
    dvred(small[:, 16:17], scr1[:, 0:64], ALU.add)
    dvtt(scr1[:, 0:64], ltile[:, 128:192], ltile[:, 192:256], ALU.mult)
    dL = dvred(small[:, 17:18], scr1[:, 0:64], ALU.add, inc=s_dve)
    act(small[:, 18:19], small[:, 16:17], AF.Exp, waits=[(s_dve, dL)])
    aL2 = act(small[:, 19:20], small[:, 17:18], AF.Exp, inc=s_act)
    dvtt(small[:, 20:21], small[:, 18:19], small[:, 19:20], ALU.subtract,
         waits=[(s_act, aL2)])
    dvts(lamb[:, 0:1], small[:, 20:21], LAMBDA_INIT, None, ALU.add, inc=s_dve)

    if "lraw" in dbg:
        ld(dbg_d["lraw"], lraw, waits=[(s_gd, P0_GD)])
    if "ltile" in dbg:
        ld(dbg_d["ltile"], ltile, waits=[(s_act, aL)])
    if "lsmall" in dbg:
        LAMD = s_dve.n
        ld(dbg_d["lsmall"], small[:, :], waits=[(s_dve, LAMD)])
    peS = pemm(ps_c[:, 1024:1152], co1f[:, :], sraw, waits=[(s_act, aL)], inc=s_pe)
    aS = act(sublb[:, :], ps_c[:, 1024:1152], AF.Copy, waits=[(s_pe, peS)], inc=s_act)

    aSilb = act(csilb, csilb, AF.Silu,
                waits=[(s_ld, P0_LD), (s_gd, P0_GD)], inc=s_act)
    tr_done = []
    for k in range(16):
        slot = TSLOT[k % 4] + 512
        w = [(s_act, aSilb), (s_pe, peS)]
        if k >= 4:
            w.append((s_act, tr_done[k - 4]))
        pk = petr(ps_cb[:, slot:slot + 2], csilb[0:2, k * 128:(k + 1) * 128], cid_b[0:2, 0:2],
                  waits=w, inc=s_pe)
        tr_done.append(act(silct[:, 2 * k:2 * k + 2], ps_cb[:, slot:slot + 2], AF.Copy,
                           waits=[(s_pe, pk)], inc=s_act))
    aSilt = tr_done[-1]
    mod_slices = [(ps_a, 0), (ps_a, 512), (ps_b, 0), (ps_b, 512), (ps_c, 0), (ps_c, 512)]
    mod_pe = []
    for k in range(16):
        slot = (k % 2) * 3072
        w = [(s_gd, GD_WMOD)]
        if k >= 2:
            w.append((s_pe, mod_pe[k - 2]))
        gk = ld(strm[:, slot:slot + 3072], wmod_bf.ap()[k * 128:(k + 1) * 128, :],
                waits=w)
        for nb in range(6):
            tgt, off = mod_slices[nb]
            pemm(tgt[0:2, off:off + 512], silct[:, 2 * k:2 * k + 2],
                 strm[:, slot + nb * 512:slot + (nb + 1) * 512],
                 start=(k == 0), stop=(k == 15),
                 waits=([(s_ld, gk), (s_act, aSilt)] if nb == 0 else []),
                 inc=(s_pe if nb == 5 else None))
        mod_pe.append(s_pe.n)
    mod_ld = 0
    for nb in range(6):
        tgt, off = mod_slices[nb]
        w_ev = [(s_pe, mod_pe[-1])]
        if nb > 0:
            w_ev.append((s_gd, mod_ld))
        aM = act(scr1[0:1, 0:512], tgt[0:1, off:off + 512], AF.Copy,
                 waits=w_ev, inc=s_act)
        mod_ld = ld(mod_cin.ap()[:, nb * 512:(nb + 1) * 512], scr1[0:1, 0:512],
                    q="pool", waits=[(s_act, aM)])
    E("pool", lambda e: e.collective_compute(
        "AllGather", ALU.bypass, replica_groups=GROUPS,
        ins=[mod_cin.ap().opt()], outs=[mod_out.ap().opt()]),
      waits=[(s_gd, mod_ld)], inc=s_cc)
    CC_MOD = s_cc.n
    mm_ld = ld(modm[:, :], mod_out.ap().rearrange("r (g p) -> (r g) p", p=128),
               q="pool", waits=[(s_cc, CC_MOD)])
    peMV = petr(ps_c[:, 1024:1120], modm[:, :], cid_f[0:96, 0:96],
                waits=[(s_gd, mm_ld), (s_act, aS)], inc=s_pe)
    aMV = act(modv[:, :], ps_c[:, 1024:1120], AF.Copy, waits=[(s_pe, peMV)], inc=s_act)
    dvts(camod[:, 0:16], modv[:, 16:32], 1.0, None, ALU.add, waits=[(s_act, aMV)])
    dvcp(camod[:, 16:32], modv[:, 0:16])
    dvts(camod[:, 32:48], modv[:, 64:80], 1.0, None, ALU.add)
    dvcp(camod[:, 48:64], modv[:, 48:64])
    for (vsrc, vdst) in ((32, 64), (80, 80)):
        dvts(scr1[:, 0:16], modv[:, vsrc:vsrc + 16], -1.0, None, ALU.mult)
        dvtt(scr1[:, 16:32], modv[:, vsrc:vsrc + 16], scr1[:, 0:16], ALU.max)
        dvts(scr1[:, 32:48], scr1[:, 16:32], 0.25, None, ALU.max)
        dvts(scr1[:, 48:64], modv[:, vsrc:vsrc + 16], 0.0, None, ALU.is_ge)
        dvts(scr1[:, 0:16], scr1[:, 48:64], 2.0, 1.0, ALU.mult, ALU.subtract)
        dvtt(camod[:, vdst:vdst + 16], scr1[:, 32:48], scr1[:, 0:16], ALU.mult)
    DVE_MOD = dvmemset(scr1[:, 63:64], 0.0, inc=s_dve)

    reg_kf = reg_k.bitcast(F32)
    h1 = reg_h
    qbuf = reg_q
    JUNK = strm[:, 8192:9216]
    p1_evac = []
    for tt in range(8):
        xst = reg_kf[:, (tt % 2) * 2048:(tt % 2) * 2048 + 2048]
        w = [(s_dve, p1_evac[tt - 2])] if tt >= 2 else []
        xl = ld(xst, x_d[tt * 128:(tt + 1) * 128, :], waits=w)
        P_ = (tt % 2) * 16
        act(reg_e[:, 0:2048], xst, AF.Square, accum=small[:, P_ + 3:P_ + 4],
            waits=[(s_ld, xl)])
        act(reg_e[:, 2048:4096], xst, AF.Copy, accum=small[:, P_ + 2:P_ + 3])
        aST = act(small[:, 10:11], small[:, 8:9], AF.Copy, inc=s_act)
        dvts(small[:, P_ + 0:P_ + 1], small[:, P_ + 2:P_ + 3], 1.0 / D, None, ALU.mult,
             waits=[(s_act, aST)])
        dvts(small[:, P_ + 4:P_ + 5], small[:, P_ + 3:P_ + 4], 1.0 / D, None, ALU.mult)
        dvtt(small[:, P_ + 5:P_ + 6], small[:, P_ + 0:P_ + 1], small[:, P_ + 0:P_ + 1],
             ALU.mult)
        dV = dvtt(small[:, P_ + 6:P_ + 7], small[:, P_ + 4:P_ + 5],
                  small[:, P_ + 5:P_ + 6], ALU.subtract, inc=s_dve)
        act(small[:, P_ + 5:P_ + 6], small[:, P_ + 6:P_ + 7], AF.Ln,
            bias=small[:, 8:9], waits=[(s_dve, dV)])
        aR = act(small[:, P_ + 1:P_ + 2], small[:, P_ + 5:P_ + 6], AF.Exp, scale=-0.5,
                 inc=s_act)
        dXN = dvts(xn_s[:, :], xst, small[:, P_ + 0:P_ + 1], small[:, P_ + 1:P_ + 2],
                   ALU.subtract, ALU.mult, waits=[(s_act, aR)], inc=s_dve)
        evac_hist = []
        for k in range(16):
            slot = TSLOT[2 + (k % 2)]
            w2 = [(s_dve, dXN), (s_act, aMV)]
            if k >= 2:
                w2.append((s_dve, evac_hist[k - 2]))
            pT = petr(ps_cb[:, slot:slot + 128], xn_s[:, k * 128:(k + 1) * 128],
                      cid_b[:, :], waits=w2, inc=s_pe)
            evac_hist.append(
                dvts(h1[:, k * 1024 + tt * 128:k * 1024 + tt * 128 + 128],
                     ps_cb[:, slot:slot + 128],
                     camod[:, k:k + 1], camod[:, 16 + k:16 + k + 1], ALU.mult, ALU.add,
                     waits=[(s_pe, pT), (s_dve, DVE_MOD)], inc=s_dve))
        p1_evac.append(s_dve.n)
        P1_ACT = aST
        if tt == 7 and "h1" in dbg:
            ld(dbg_d["h1"], h1[:, :], waits=[(s_dve, s_dve.n)])
        if tt == 7 and "p1small" in dbg:
            ld(dbg_d["p1small"], small[:, :], waits=[(s_dve, dXN)])
        if tt == 7 and "p1xn" in dbg:
            ld(dbg_d["p1xn"], xn_s[:, :], waits=[(s_dve, dXN)])
    P1_DVE = s_dve.n

    CQ = reg_k[:, ROPE0:ROPE0 + 1024]
    SQ = reg_k[:, ROPE0 + 1024:ROPE0 + 2048]
    CK = reg_k[:, ROPE0 + 2048:ROPE0 + 3072]
    SK = reg_k[:, ROPE0 + 3072:ROPE0 + 4096]
    kst = [reg_k[:, 12288:13312], reg_k[:, 13312:14336]]
    vtm_view = kv_cin.ap()[2048:4096, :].rearrange("(t two) c -> t (two c)", two=2)

    for (src, dst) in ((wq_d, wq_bf), (wk_d, wk_bf), (wv_d, wv_bf), (wo_d, wo_bf)):
        E("pool", lambda e, s=src, d=dst: e.dma_start(out=d.ap()[:, :], in_=s),
          inc=s_gd, inck=16)
    GD_WO = s_gd.n
    GD_W = {"q": GD_WO - 3, "k": GD_WO - 2, "v": GD_WO - 1}

    prev_w_pe = 0
    P2_done = {"act": 0, "dve": 0}
    kv_st = []
    for wi, wname in enumerate(("q", "k", "v")):
        wbf = (wq_bf, wk_bf, wv_bf)[wi]
        wts = [(s_gd, GD_W[wname])] + ([(s_pe, prev_w_pe)] if wi else
                                             [(s_act, P1_ACT), (s_act, aSilt),
                                              (s_pe, mod_pe[-1])])
        for k in range(16):
            ldt(reg_e[:, k * 2048:(k + 1) * 2048], wbf.ap()[:, k * 128:(k + 1) * 128],
                waits=(wts if k == 0 else []))
        trd = s_ld.n
        evac_act = []
        evac_dve = []
        vstores = []
        for m in range(16):
            tgt = ps_a if m % 2 == 0 else ps_b
            w_p = [(s_ld, trd)]
            if m >= 2:
                w_p.append((s_act, evac_act[m - 2]))
            elif wi > 0:
                w_p.append((s_act, P2_done["act"]))
            if wi == 0 and m == 0:
                w_p.append((s_dve, P1_DVE))
            for nh in range(2):
                for k in range(16):
                    pemm(tgt[:, nh * 512:(nh + 1) * 512],
                         reg_e[:, k * 2048 + m * 128:k * 2048 + m * 128 + 128],
                         h1[:, k * 1024 + nh * 512:k * 1024 + nh * 512 + 512],
                         start=(k == 0), stop=(k == 15),
                         waits=(w_p if (nh == 0 and k == 0) else []),
                         inc=(s_pe if (nh == 1 and k == 15) else None))
            mmd = s_pe.n
            w_e = [(s_pe, mmd)]
            if evac_dve:
                w_e.append((s_dve, evac_dve[-1]))
            elif wi > 0:
                w_e.append((s_dve, P2_done["dve"]))
            aQ = act(qraw[:, :], tgt[:, :], AF.Copy, waits=w_e, inc=s_act)
            evac_act.append(aQ)
            if wname in ("q", "k"):
                coff = (m % 2) * 1024
                w_s = [(s_act, aQ)]
                if m >= 2:
                    w_s.append((s_dve, evac_dve[m - 2]))
                elif wi > 0:
                    w_s.append((s_dve, P2_done["dve"]))
                for nh in range(2):
                    pemm(ps_c[:, coff + nh * 512:coff + (nh + 1) * 512],
                         cpsw[:, :], qraw[:, nh * 512:(nh + 1) * 512],
                         waits=(w_s if nh == 0 else []),
                         inc=(s_pe if nh == 1 else None))
                swd = s_pe.n
                Ct, St = (CQ, SQ) if wname == "q" else (CK, SK)
                dvtt(scr1[:, :], qraw[:, :], Ct, ALU.mult, waits=[(s_act, aQ)])
                dvtt(scr2[:, :], ps_c[:, coff:coff + 1024], St, ALU.mult,
                     waits=[(s_pe, swd)])
                dstt = qbuf[:, m * 1024:(m + 1) * 1024] if wname == "q" else kst[m % 2]
                dR = dvtt(dstt, scr1[:, :], scr2[:, :], ALU.add, inc=s_dve)
                if wname == "k":
                    kv_st.append(ld(kv_cin.ap()[m * 128:(m + 1) * 128, :], dstt,
                                    waits=[(s_dve, dR)]))
                evac_dve.append(dR)
            else:
                for tj in range(8):
                    slot = TSLOT[tj % 4]
                    w_t = [(s_act, aQ)]
                    if len(vstores) >= 4:
                        w_t.append((s_act, vstores[-4][0]))
                    else:
                        w_t.append((s_dve, P2_done["dve"]))
                    pT = petr(ps_cb[:, slot:slot + 128],
                              qraw[:, tj * 128:(tj + 1) * 128], cid_b[:, :],
                              waits=w_t, inc=s_pe)
                    w_c = [(s_pe, pT)]
                    if len(vstores) >= 2:
                        w_c.append((s_ld, vstores[-2][1]))
                    aP = act(piece[:, (tj % 2) * 128:(tj % 2) * 128 + 128],
                             ps_cb[:, slot:slot + 128], AF.Copy, waits=w_c, inc=s_act)
                    stn = ld(vtm_view[tj * 128:(tj + 1) * 128, m * 128:(m + 1) * 128],
                             piece[:, (tj % 2) * 128:(tj % 2) * 128 + 128],
                             waits=[(s_act, aP)])
                    vstores.append((aP, stn))
                    kv_st.append(stn)
                evac_dve.append(s_dve.n)
        prev_w_pe = s_pe.n
        P2_done = {"act": s_act.n, "dve": s_dve.n}

    KV_LD = s_ld.n

    E("pool", lambda e: e.collective_compute(
        "AllGather", ALU.bypass, replica_groups=GROUPS,
        ins=[kv_cin.ap().opt()], outs=[kv_out.ap().opt()]),
      waits=[(s_ld, KV_LD)], inc=s_cc)
    CC_KV = s_cc.n
    QN_DVE = s_dve.n

    expb = [reg_e[:, 0:16384], reg_e[:, 16384:32768]]
    Kb = [reg_k[:, 0:4096], reg_k[:, 4096:8192]]
    Vb = [reg_k[:, 8192:12320], reg_k[:, 12320:16448]]
    VM0 = dvmemset(Vb[0], 1.0, waits=[(s_dve, P2_done["dve"]), (s_act, P2_done["act"])],
                   inc=s_dve)
    VM1 = dvmemset(Vb[1], 1.0, inc=s_dve)

    def kvload(vh, waits):
        par = vh % 2
        w = list(waits)
        for r in range(4):
            ld(Kb[par][:, r * 1024:(r + 1) * 1024],
               kv_out.ap()[r * 4097 + vh * 128:r * 4097 + (vh + 1) * 128, :],
               waits=(w if r == 0 else []))
        for r in range(4):
            src = kv_out.ap()[r * 4097 + 2048:r * 4097 + 4096, :] \
                .rearrange("(t two) c -> t (two c)", two=2)[:, vh * 128:(vh + 1) * 128] \
                .rearrange("(tc p) dv -> p tc dv", p=128)
            dstv = Vb[par][:, r * 1032:(r + 1) * 1032] \
                .rearrange("p (tc c) -> p tc c", c=129)[:, :, 0:128]
            ld(dstv, src)
        return s_ld.n

    kv_ld_n = [0] * 17
    kv_ld_n[0] = kvload(0, [(s_cc, CC_KV), (s_dve, VM0)])
    kv_ld_n[1] = kvload(1, [(s_dve, VM1)])

    phase_pe_end = []
    exp_inc = {}
    evac_acts = []
    last_pair_act = [P2_done["act"], P2_done["act"]]
    scps = [ps_a, ps_b]
    epi_dve_last = QN_DVE
    aRs_par = [0, 0]
    negrow = xn_s
    negrowB = xn_s
    mrow = logit
    NEGM_LD = [0, 0]
    EVAC_ACT = [0]
    MAXST_DVE = [QN_DVE, QN_DVE]

    MAXB = ps_c[:, 1536:2048]
    MAXB_T = ps_cb[0:1, 3072:3200]

    negm_stg = piece[0:16, 0:128]

    def max_stage_steps(vh, waits):
        par = vh % 2
        w0 = list(waits)
        last_red = 0
        for ui in range(16):
            hhp, qt = ui % 2, ui // 2
            for scq in range(8):
                w_u = []
                if ui == 0 and scq == 0:
                    w_u = list(w0)
                if last_red:
                    w_u.append((s_dve, last_red))
                pemm(MAXB[:, 0:512],
                     qbuf[hhp * 64:hhp * 64 + 64,
                          vh * 1024 + qt * 128:vh * 1024 + qt * 128 + 128],
                     Kb[par][hhp * 64:hhp * 64 + 64, scq * 512:(scq + 1) * 512],
                     waits=w_u, inc=s_pe)
                pe_u = s_pe.n
                if scq == 0:
                    last_red = dvred(mrow[:, 0:1], MAXB[:, 0:512], ALU.max,
                                     waits=[(s_pe, pe_u)], inc=s_dve)
                else:
                    dvred(mrow[:, 1:2], MAXB[:, 0:512], ALU.max,
                          waits=[(s_pe, pe_u)])
                    last_red = dvtt(mrow[:, 0:1], mrow[:, 0:1], mrow[:, 1:2],
                                    ALU.max, inc=s_dve)
                yield
            dvts(qraw[:, hhp * 8 + qt:hhp * 8 + qt + 1], mrow[:, 0:1], -1.0,
                 None, ALU.mult, inc=s_dve)
            last_red = s_dve.n
            yield
        pT = petr(ps_cb[0:16, 3072:3200], qraw[:, 0:16], cid_b[:, :],
                  waits=[(s_dve, last_red)], inc=s_pe)
        yield
        w_e = [(s_pe, pT)]
        if NEGM_LD[0] or NEGM_LD[1]:
            w_e.append((s_ld, max(NEGM_LD)))
        evac = act(negm_stg, ps_cb[0:16, 3072:3200], AF.Copy, waits=w_e, inc=s_act)
        EVAC_ACT[0] = evac
        st1 = ld(negm_dram.ap().rearrange("o (u c) -> (o u) c", c=128),
                 negm_stg[:, :], waits=[(s_act, evac)])
        yield
        off = par * 1024
        ld(negrow[0:1, off:off + 1024], negm_dram.ap()[0:1, 0:1024],
           waits=[(s_ld, st1)])
        ld(negrowB[64:65, off:off + 1024], negm_dram.ap()[0:1, 1024:2048])
        NEGM_LD[par] = s_ld.n
        MAXST_DVE[par] = s_dve.n

    exp_last = [P2_done["act"], P2_done["act"]]
    item_pe_end = []
    for _ in max_stage_steps(0, [(s_ld, kv_ld_n[0]), (s_dve, P2_done["dve"]),
                                 (s_act, P2_done["act"])]):
        pass
    if "negm" in dbg:
        ld(dbg_d["negm"], negrow[0:2, :], waits=[(s_ld, NEGM_LD[0])])
    max_iter = [None]
    for item in range(32):
        vh, qh = item // 2, item % 2
        par = vh % 2
        if qh == 0 and vh + 1 < 16:
            war = item_pe_end[-1] if item_pe_end else 0
            kv_ld_n[vh + 1] = kvload(vh + 1, ([(s_pe, war)] if war else []))
            w_ms = [(s_ld, kv_ld_n[vh + 1])]
            if EVAC_ACT[0]:
                w_ms.append((s_act, EVAC_ACT[0]))
            if item_pe_end:
                w_ms.append((s_pe, item_pe_end[-1]))
            if item >= 2:
                w_ms.append((s_pe, item_pe_end[-1]))
            max_iter[0] = max_stage_steps(vh + 1, w_ms)
        exp_war = item_pe_end[item - 2] if item >= 2 else 0
        qoff = vh * 1024 + qh * 512
        for p in range(16):
            for half in range(2):
                sc = p * 2 + half
                for hhp in range(2):
                    w_sc = []
                    if half == 0 and hhp == 0:
                        w_sc = [(s_act, exp_last[0]), (s_act, exp_last[1])]
                        if p == 0:
                            w_sc += [(s_ld, kv_ld_n[vh]), (s_dve, MAXST_DVE[par]),
                                     (s_ld, NEGM_LD[par]), (s_dve, ONES33)]
                            if exp_war:
                                w_sc.append((s_pe, exp_war))
                    pemm(scps[hhp][:, half * 512:(half + 1) * 512],
                         Kb[par][hhp * 64:hhp * 64 + 64, sc * 128:(sc + 1) * 128],
                         qbuf[hhp * 64:hhp * 64 + 64, qoff:qoff + 512],
                         start=True, stop=False, waits=w_sc)
                if half == 1 and max_iter[0] is not None:
                    try:
                        next(max_iter[0])
                    except StopIteration:
                        max_iter[0] = None
            bcol = par * 1024 + qh * 512
            for half in range(2):
                for hhp in range(2):
                    pemm(scps[hhp][:, half * 512:(half + 1) * 512],
                         ones65[0:1, :] if hhp == 0 else ones65[64:65, :],
                         negrow[0:1, bcol:bcol + 512] if hhp == 0 else
                         negrowB[64:65, bcol:bcol + 512],
                         start=False, stop=True,
                         inc=(s_pe if (half == 1 and hhp == 1) else None))
            pe_sc = s_pe.n
            def mstep(k=1):
                if max_iter[0] is not None:
                    for _ in range(k):
                        try:
                            next(max_iter[0])
                        except StopIteration:
                            max_iter[0] = None
                            break
            mstep()
            if p >= 1:
                w_v = [(s_act, exp_inc[(0, p - 1)]), (s_act, exp_inc[(1, p - 1)])]
                if p == 1 and evac_acts:
                    w_v.append((s_act, evac_acts[-1]))
                first = True
                for half in range(2):
                    sc = (p - 1) * 2 + half
                    for qt in range(4):
                        for hhp in range(2):
                            so = NUMOFF[hhp * 4 + qt]
                            pemm(ps_c[:, so:so + 129],
                                 expb[hhp][:, sc * 512 + qt * 128:sc * 512 + qt * 128 + 128],
                                 Vb[par][:, sc * 129:(sc + 1) * 129],
                                 start=(sc == 0), stop=False,
                                 waits=(w_v if first else []))
                            first = False
                        if qt in (1, 3):
                            mstep()
            mstep(2 if qh == 1 else 1)
            for hhp in range(2):
                w_e = [(s_pe, pe_sc)]
                exp_inc[(hhp, p)] = act(expb[hhp][:, p * 1024:(p + 1) * 1024],
                                        scps[hhp][:, :], AF.Exp, waits=w_e, inc=s_act)
                exp_last[hhp] = exp_inc[(hhp, p)]
        w_v = [(s_act, exp_inc[(0, 15)]), (s_act, exp_inc[(1, 15)])]
        first = True
        for half in range(2):
            sc = 30 + half
            for qt in range(4):
                for hhp in range(2):
                    so = NUMOFF[hhp * 4 + qt]
                    pemm(ps_c[:, so:so + 129],
                         expb[hhp][:, sc * 512 + qt * 128:sc * 512 + qt * 128 + 128],
                         Vb[par][:, sc * 129:(sc + 1) * 129],
                         start=False, stop=(sc == 31),
                         waits=(w_v if first else []),
                         inc=(s_pe if (half == 1 and qt == 3 and hhp == 1) else None))
                    first = False
        item_pe_end.append(s_pe.n)
        phase_pe_end.append(s_pe.n)
        if qh == 1 and max_iter[0] is not None:
            for _ in max_iter[0]:
                pass
            max_iter[0] = None
        aEv = act(numcp[:, :], ps_c[:, 0:1536], AF.Copy,
                  waits=[(s_pe, item_pe_end[-1]), (s_dve, epi_dve_last)], inc=s_act)
        evac_acts.append(aEv)
        if item == 0 and "numcp" in dbg:
            ld(dbg_d["numcp"], numcp[:, :], waits=[(s_act, aEv)])
        for qt in range(4):
            a0, aB, aC = attbuf[qt % 2]
            oA, oB = NUMOFF[qt], NUMOFF[4 + qt]
            dvrecip(small[:, 20:21], numcp[:, oA + 128:oA + 129], waits=[(s_act, aEv)])
            dvrecip(small[:, 21:22], numcp[:, oB + 128:oB + 129])
            dvtt(small[:, 21:22], small[:, 21:22], lamb[:, 0:1], ALU.mult)
            dvts(a0[:, :], numcp[:, oA:oA + 128], small[:, 20:21], None, ALU.mult,
                 waits=[(s_act, aRs_par[qt % 2])])
            dvts(aB[:, :], numcp[:, oB:oB + 128], small[:, 21:22], None, ALU.mult)
            dSb = dvtt(a0[:, :], a0[:, :], aB[:, :], ALU.subtract, inc=s_dve)
            act(JUNK[:, 0:128], a0[:, :], AF.Square, accum=small[:, 22:23],
                waits=[(s_dve, dSb)])
            act(small[:, 23:24], small[:, 22:23], AF.Ln, scale=1.0 / 128.0,
                bias=small[:, 8:9])
            aRs = act(small[:, 24:25], small[:, 23:24], AF.Exp, scale=-0.5, inc=s_act)
            aRs_par[qt % 2] = aRs
            dvts(aC[:, :], a0[:, :], small[:, 24:25], None, ALU.mult,
                 waits=[(s_act, aRs)])
            col = vh * 1024 + qh * 512 + qt * 128
            epi_dve_last = dvtt(reg_h[:, col:col + 128], aC[:, :], sublb[:, :],
                                ALU.mult, inc=s_dve)
    P3_PE = s_pe.n
    EPI_DVE = epi_dve_last

    tp_acts = []
    for j in range(128):
        slot = TSLOT[j % 4]
        w_t = [(s_dve, EPI_DVE)]
        if j >= 4:
            w_t = [(s_act, tp_acts[j - 4])]
        pT = petr(ps_cb[:, slot:slot + 128], reg_h[:, j * 128:(j + 1) * 128],
                  cid_b[:, :], waits=w_t, inc=s_pe)
        tp_acts.append(act(reg_h[:, j * 128:(j + 1) * 128], ps_cb[:, slot:slot + 128],
                           AF.Copy, waits=[(s_pe, pT)], inc=s_act))
    ATTNT_ACT = tp_acts[-1]
    if "attnt" in dbg:
        ld(dbg_d["attnt"], reg_h[:, :], waits=[(s_act, ATTNT_ACT)])

    for k in range(16):
        ldt(reg_e[:, k * 2048:(k + 1) * 2048], wo_bf.ap()[:, k * 128:(k + 1) * 128],
            waits=([(s_gd, GD_WO), (s_pe, P3_PE)] if k == 0 else []))
    WOT_LD = s_ld.n
    h2 = reg_q
    wo_slots = [(ps_a, 0), (ps_a, 512), (ps_b, 0), (ps_b, 512)]
    wo_evac = [0] * 4
    x1_hist = []
    h2_hist = []
    x1_st = []
    for tt in range(8):
        xst = reg_kf[:, (tt % 2) * 2048:(tt % 2) * 2048 + 2048]
        w = [(s_pe, P3_PE)]
        if tt >= 2:
            w.append((s_dve, x1_hist[tt - 2]))
        xl = ld(xst, x_d[tt * 128:(tt + 1) * 128, :], waits=w)
        x1t = reg_kf[:, 4096:6144]
        for m in range(16):
            tgt, off = wo_slots[m % 4]
            w_m = [(s_ld, WOT_LD), (s_act, ATTNT_ACT)]
            if wo_evac[m % 4]:
                w_m.append((s_dve, wo_evac[m % 4]))
            if tt > 0 and m == 0:
                w_m.append((s_dve, h2_hist[-1]))
            for k in range(16):
                pemm(tgt[:, off:off + 128],
                     reg_e[:, k * 2048 + m * 128:k * 2048 + m * 128 + 128],
                     reg_h[:, k * 1024 + tt * 128:k * 1024 + tt * 128 + 128],
                     start=(k == 0), stop=(k == 15),
                     waits=(w_m if k == 0 else []),
                     inc=(s_pe if k == 15 else None))
            mmd = s_pe.n
            dG = dvts(piece[:, (m % 2) * 128:(m % 2) * 128 + 128], tgt[:, off:off + 128],
                      camod[:, 64 + m:65 + m], None, ALU.mult,
                      waits=[(s_pe, mmd), (s_ld, xl)], inc=s_dve)
            slot = TSLOT[m % 4]
            pT = petr(ps_cb[:, slot:slot + 128], piece[:, (m % 2) * 128:(m % 2) * 128 + 128],
                      cid_b[:, :], waits=[(s_dve, dG)], inc=s_pe)
            w_x1 = [(s_pe, pT)]
            if m == 0 and x1_st:
                w_x1.append((s_ld, x1_st[-1]))
            wo_evac[m % 4] = dvtt(x1t[:, m * 128:(m + 1) * 128],
                                  xst[:, m * 128:(m + 1) * 128],
                                  ps_cb[:, slot:slot + 128], ALU.add,
                                  waits=w_x1, inc=s_dve)
        dX1 = s_dve.n
        P_ = (tt % 2) * 16
        act(xn_s[:, :], x1t[:, :], AF.Square, accum=small[:, P_ + 3:P_ + 4],
            waits=[(s_dve, dX1)])
        act(xn_s[:, :], x1t[:, :], AF.Copy, accum=small[:, P_ + 2:P_ + 3])
        aS2 = act(small[:, 10:11], small[:, 8:9], AF.Copy, inc=s_act)
        dvts(small[:, P_ + 0:P_ + 1], small[:, P_ + 2:P_ + 3], 1.0 / D, None, ALU.mult,
             waits=[(s_act, aS2)])
        dvts(small[:, P_ + 4:P_ + 5], small[:, P_ + 3:P_ + 4], 1.0 / D, None, ALU.mult)
        dvtt(small[:, P_ + 5:P_ + 6], small[:, P_ + 0:P_ + 1], small[:, P_ + 0:P_ + 1],
             ALU.mult)
        dV2 = dvtt(small[:, P_ + 6:P_ + 7], small[:, P_ + 4:P_ + 5],
                   small[:, P_ + 5:P_ + 6], ALU.subtract, inc=s_dve)
        act(small[:, P_ + 5:P_ + 6], small[:, P_ + 6:P_ + 7], AF.Ln,
            bias=small[:, 8:9], waits=[(s_dve, dV2)])
        aR2 = act(small[:, P_ + 1:P_ + 2], small[:, P_ + 5:P_ + 6], AF.Exp, scale=-0.5,
                  inc=s_act)
        dXN2 = dvts(xn_s[:, :], x1t[:, :], small[:, P_ + 0:P_ + 1],
                    small[:, P_ + 1:P_ + 2], ALU.subtract, ALU.mult,
                    waits=[(s_act, aR2)], inc=s_dve)
        ev2 = []
        for k in range(16):
            slot = TSLOT[k % 4]
            w2 = [(s_dve, dXN2)]
            if k >= 4:
                w2 = [(s_dve, ev2[k - 4])]
            pT = petr(ps_cb[:, slot:slot + 128], xn_s[:, k * 128:(k + 1) * 128],
                      cid_b[:, :], waits=w2, inc=s_pe)
            ev2.append(dvts(h2[:, k * 1024 + tt * 128:k * 1024 + tt * 128 + 128],
                            ps_cb[:, slot:slot + 128],
                            camod[:, 32 + k:33 + k], camod[:, 48 + k:49 + k],
                            ALU.mult, ALU.add, waits=[(s_pe, pT)], inc=s_dve))
        h2_hist.append(s_dve.n)
        x1_hist.append(s_dve.n)
        x1_st.append(ld(x1_dram.ap()[tt * 128:(tt + 1) * 128, :], x1t,
                        waits=[(s_dve, x1_hist[-1])]))
    H2_DVE = s_dve.n
    X1_LD = s_ld.n
    WOUT_PE = s_pe.n
    if "h2" in dbg:
        ld(dbg_d["h2"], h2[:, :], waits=[(s_dve, H2_DVE)])

    rt_done = []
    for tt in range(8):
        w_r = [(s_dve, H2_DVE)]
        if tt >= 2:
            w_r.append((s_dve, rt_done[tt - 2]))
        for k in range(16):
            pemm(ps_c[:, (tt % 2) * 512:(tt % 2) * 512 + 8],
                 h2[:, k * 1024 + tt * 128:k * 1024 + tt * 128 + 128],
                 wrtb[:, k * 8:(k + 1) * 8],
                 start=(k == 0), stop=(k == 15),
                 waits=(w_r if k == 0 else []),
                 inc=(s_pe if k == 15 else None))
        rmm = s_pe.n
        w_lg = [(s_pe, rmm)]
        if rt_done:
            w_lg.append((s_dve, rt_done[-1]))
        aLg = act(logit[:, 0:8], ps_c[:, (tt % 2) * 512:(tt % 2) * 512 + 8], AF.Copy,
                  waits=w_lg, inc=s_act)
        dvred(small[:, 25:26], logit[:, 0:8], ALU.max, waits=[(s_act, aLg)])
        dvts(logit[:, 8:16], logit[:, 0:8], small[:, 25:26], None, ALU.is_equal)
        dvts(logit[:, 16:24], logit[:, 8:16], -1e30, None, ALU.mult)
        dvtt(logit[:, 8:16], logit[:, 0:8], logit[:, 16:24], ALU.add)
        dvred(small[:, 26:27], logit[:, 8:16], ALU.max)
        dM2 = dvts(small[:, 27:28], small[:, 25:26], -1.0, None, ALU.mult, inc=s_dve)
        act(logit[:, 16:24], logit[:, 0:8], AF.Exp, bias=small[:, 27:28],
            waits=[(s_dve, dM2)])
        aE2 = act(small[:, 28:29], small[:, 26:27], AF.Exp, bias=small[:, 27:28],
                  inc=s_act)
        dvts(small[:, 30:31], small[:, 28:29], 1.0, None, ALU.add, waits=[(s_act, aE2)])
        dvrecip(small[:, 29:30], small[:, 30:31])
        dvts(logit[:, 24:32], logit[:, 0:8], small[:, 26:27], None, ALU.is_ge)
        dvtt(logit[:, 24:32], logit[:, 24:32], logit[:, 16:24], ALU.mult)
        rt_done.append(dvts(wtok[:, tt * 8:(tt + 1) * 8], logit[:, 24:32],
                            small[:, 29:30], None, ALU.mult, inc=s_dve))
    RT_DVE = rt_done[-1]
    WTR_ACT = s_act.n
    if "wtok" in dbg:
        ld(dbg_d["wtok"], wtok[:, :], waits=[(s_dve, RT_DVE)])

    ACC0 = dvmemset(reg_h[:, :], 0.0, waits=[(s_pe, WOUT_PE)], inc=s_dve)
    h2mid = reg_k[:, 0:16384]
    fcslot = [reg_e[:, 0:8192], reg_e[:, 8192:16384]]
    pjslot = [reg_e[:, 16384:24576], reg_e[:, 24576:32768]]
    RELU = strm[:, 8192:9216]
    fc_pe = []
    fc_sq = []
    pj_pe = []
    acc_dve = []
    wtb_act = 0
    for e in range(8):
        wrow6 = numcp[0:1, 1024:1152]
        wtb_act = 0
        for tt in range(8):
            w_w = [(s_dve, RT_DVE)]
            if acc_dve:
                w_w.append((s_dve, acc_dve[-1]))
            if wtb_act:
                w_w.append((s_act, wtb_act))
            pT = petr(ps_c[0:1, 1536 + (tt % 2) * 128:1536 + (tt % 2) * 128 + 128],
                      wtok[:, tt * 8 + e:tt * 8 + e + 1], cid_f[:, :],
                      waits=w_w, inc=s_pe)
            aRow = act(wrow6, ps_c[0:1, 1536 + (tt % 2) * 128:1536 + (tt % 2) * 128 + 128],
                       AF.Copy, waits=[(s_pe, pT)], inc=s_act)
            peB = pemm(ps_c[:, (tt % 2) * 512:(tt % 2) * 512 + 128], co1f[:, :], wrow6,
                       waits=[(s_act, aRow)], inc=s_pe)
            wtb_act = act(wtb[:, tt * 128:(tt + 1) * 128],
                          ps_c[:, (tt % 2) * 512:(tt % 2) * 512 + 128],
                          AF.Copy, waits=[(s_pe, peB)], inc=s_act)
        for g in range(4):
            slot = g % 2
            w_d = [(s_pe, WOUT_PE)]
            if len(fc_pe) >= 2:
                w_d = [(s_pe, fc_pe[-2])]
            E("pool", lambda en, sl=slot, ee=e, gg=g: en.dma_start(
                out=fcslot[sl].rearrange("p (k d) -> p k d", d=512),
                in_=wfc_d[ee, :, gg * 512:(gg + 1) * 512]
                .rearrange("(k p) d -> p k d", p=128)),
              waits=w_d, inc=s_gd, inck=16)
            gdn = s_gd.n
            for dmi in range(4):
                dm = g * 4 + dmi
                tgt = ps_a if dm % 2 == 0 else ps_b
                w_f = [(s_gd, gdn), (s_dve, ACC0)]
                if len(fc_sq) >= 2:
                    w_f.append((s_act, fc_sq[-2]))
                if e > 0 and dm == 0 and pj_pe:
                    w_f.append((s_pe, pj_pe[-1]))
                for nh in range(2):
                    for k in range(16):
                        pemm(tgt[:, nh * 512:(nh + 1) * 512],
                             fcslot[slot][:, k * 512 + dmi * 128:k * 512 + dmi * 128 + 128],
                             h2[:, k * 1024 + nh * 512:k * 1024 + nh * 512 + 512],
                             start=(k == 0), stop=(k == 15),
                             waits=(w_f if (nh == 0 and k == 0) else []),
                             inc=(s_pe if (nh == 1 and k == 15) else None))
                fc_pe.append(s_pe.n)
                w_rel = [(s_pe, fc_pe[-1])]
                if fc_sq:
                    w_rel.append((s_act, fc_sq[-1]))
                dRel = dvts(RELU, tgt[:, :], 0.0, None, ALU.max, waits=w_rel, inc=s_dve)
                w_sq = [(s_dve, dRel)]
                if e > 0 and pj_pe:
                    w_sq.append((s_pe, pj_pe[-1]))
                fc_sq.append(act(h2mid[:, dm * 1024:(dm + 1) * 1024], RELU, AF.Square,
                                 waits=w_sq, inc=s_act))
        FC_ACT = fc_sq[-1]
        for g in range(4):
            slot = g % 2
            w_d = [(s_pe, WOUT_PE)]
            if len(pj_pe) >= 2:
                w_d = [(s_pe, pj_pe[-2])]
            E("pool", lambda en, sl=slot, ee=e, gg=g: en.dma_start(
                out=pjslot[sl].rearrange("p (k d) -> p k d", d=512),
                in_=wpj_d[ee, :, gg * 512:(gg + 1) * 512]
                .rearrange("(k p) d -> p k d", p=128)),
              waits=w_d, inc=s_gd, inck=16)
            gdn = s_gd.n
            for cmi in range(4):
                cm = g * 4 + cmi
                tps = ps_c[:, 0:1024] if cm % 2 == 0 else ps_c[:, 1024:2048]
                w_p = [(s_gd, gdn), (s_act, FC_ACT), (s_act, wtb_act)]
                if len(acc_dve) >= 2:
                    w_p.append((s_dve, acc_dve[-2]))
                for nh in range(2):
                    for dm in range(16):
                        pemm(tps[:, nh * 512:(nh + 1) * 512],
                             pjslot[slot][:, dm * 512 + cmi * 128:dm * 512 + cmi * 128 + 128],
                             h2mid[:, dm * 1024 + nh * 512:dm * 1024 + nh * 512 + 512],
                             start=(dm == 0), stop=(dm == 15),
                             waits=(w_p if (nh == 0 and dm == 0) else []),
                             inc=(s_pe if (nh == 1 and dm == 15) else None))
                pj_pe.append(s_pe.n)
                dT = dvtt(scr1[:, :], tps[:, :], wtb[:, :], ALU.mult,
                          waits=[(s_pe, pj_pe[-1])])
                acc_dve.append(dvtt(reg_h[:, cm * 1024:(cm + 1) * 1024],
                                    reg_h[:, cm * 1024:(cm + 1) * 1024], scr1[:, :],
                                    ALU.add, inc=s_dve))
    P6_DVE = acc_dve[-1]
    P6_PE = s_pe.n
    if "acc" in dbg:
        ld(dbg_d["acc"], reg_h[:, :], waits=[(s_dve, P6_DVE)])

    GB_DVE = P6_DVE
    out_dve = []
    pg_hist = []
    for tt in range(8):
        xst = reg_kf[:, 0:2048]
        yst = reg_kf[:, 2048:4096]
        w = [(s_pe, P6_PE)]
        if tt > 0:
            w.append((s_out, tt * 16))
        xl = ld(xst, x1_dram.ap()[tt * 128:(tt + 1) * 128, :], waits=w)
        for cm in range(16):
            slot = TSLOT[cm % 4]
            w_g = [(s_dve, GB_DVE), (s_pe, P6_PE)]
            if len(out_dve) >= 2:
                w_g.append((s_pe, pg_hist[-2]))
            dg = dvts(piece[:, (cm % 2) * 128:(cm % 2) * 128 + 128],
                      reg_h[:, cm * 1024 + tt * 128:cm * 1024 + tt * 128 + 128],
                      camod[:, 80 + cm:81 + cm], None, ALU.mult,
                      waits=w_g, inc=s_dve)
            w_t = [(s_dve, dg)]
            if len(out_dve) >= 4:
                w_t.append((s_dve, out_dve[-4]))
            pT = petr(ps_cb[:, slot:slot + 128],
                      piece[:, (cm % 2) * 128:(cm % 2) * 128 + 128],
                      cid_b[:, :], waits=w_t, inc=s_pe)
            pg_hist.append(pT)
            out_dve.append(dvtt(yst[:, cm * 128:(cm + 1) * 128],
                                xst[:, cm * 128:(cm + 1) * 128],
                                ps_cb[:, slot:slot + 128], ALU.add,
                                waits=[(s_pe, pT), (s_ld, xl)], inc=s_dve))
        E("syncq", lambda e, t=tt, y=yst: e.dma_start(
            out=y_d[t * 128:(t + 1) * 128, :], in_=y),
          waits=[(s_dve, out_dve[-1])], inc=s_out, inck=16)

    if "modv" in dbg:
        ld(dbg_d["modv"], modv[:, :], waits=[(s_act, aMV)])
    if "camod" in dbg:
        ld(dbg_d["camod"], camod[:, :], waits=[(s_dve, DVE_MOD)])
    if "lamb" in dbg:
        ld(dbg_d["lamb"], lamb[:, :], waits=[(s_dve, DVE_MOD)])
    if False and "h1" in dbg:
        ld(dbg_d["h1"], h1[:, :], waits=[(s_dve, P1_DVE)])
    if "qrot" in dbg:
        ld(dbg_d["qrot"], qbuf[:, :], waits=[(s_dve, P2_done["dve"])])
    if "kcin" in dbg:
        ld(dbg_d["kcin"], kv_cin.ap()[:, :], waits=[(s_ld, KV_LD)])
    if False and "attnt" in dbg:
        ld(dbg_d["attnt"], reg_h[:, :], waits=[(s_act, ATTNT_ACT)])
    if "x1" in dbg:
        ld(dbg_d["x1"], x1_dram.ap()[:, :], waits=[(s_ld, X1_LD)])
    if False and "h2" in dbg:
        ld(dbg_d["h2"], h2[:, :], waits=[(s_dve, H2_DVE)])
    if False and "wtok" in dbg:
        ld(dbg_d["wtok"], wtok[:, :], waits=[(s_dve, RT_DVE)])
    if False and "acc" in dbg:
        ld(dbg_d["acc"], reg_h[:, :], waits=[(s_dve, P6_DVE)])

    E("syncq", lambda e: e.nop(), waits=[(s_out, s_out.n), (s_ld, s_ld.n)])

    blk = stk.enter_context(nc.Block())

    @blk.tensor
    def _(t):
        for f in sch.ops["pe"]:
            f(t)

    @blk.scalar
    def _(s):
        for f in sch.ops["act"]:
            f(s)

    @blk.vector
    def _(v):
        for f in sch.ops["dve"]:
            f(v)

    @blk.gpsimd
    def _(g):
        for f in sch.ops["pool"]:
            f(g)

    @blk.sync
    def _(sy):
        for f in sch.ops["syncq"]:
            f(sy)

    stk.close()
    nc._sched_trace = sch
    return nc, dbg_d


_BUILT = {}


def _get_program(dbg=()):
    key = tuple(sorted(dbg))
    if key not in _BUILT:
        _BUILT[key] = build_program(dbg)
    return _BUILT[key]


def make_inmaps(inputs):
    lams = np.concatenate([
        np.asarray(inputs["L0_attn_lambda_q1"]), np.asarray(inputs["L0_attn_lambda_k1"]),
        np.asarray(inputs["L0_attn_lambda_q2"]), np.asarray(inputs["L0_attn_lambda_k2"])]
    ).reshape(1, -1).astype(np.float32)
    x = np.asarray(inputs["L0_attn_x"])
    wmod = np.asarray(inputs["L0_wmod"])
    c = np.asarray(inputs["L0_c"])
    in_maps = []
    for core in range(NCORES):
        b, tq = core // 4, core % 4
        m = dict(_host_consts(core))
        m["x_sl"] = np.ascontiguousarray(x[b, tq * TC:(tq + 1) * TC, :])
        m["wq"] = np.asarray(inputs["L0_attn_wq"])
        m["wk"] = np.asarray(inputs["L0_attn_wk"])
        m["wv"] = np.asarray(inputs["L0_attn_wv"])
        m["wout"] = np.asarray(inputs["L0_attn_wout"])
        m["wfc"] = np.asarray(inputs["L0_ffn_w_fc"])
        m["wproj"] = np.asarray(inputs["L0_ffn_w_proj"])
        m["wrouter"] = np.asarray(inputs["L0_ffn_w_router"])
        m["wmod_sl"] = np.ascontiguousarray(wmod[:, tq * MODW:(tq + 1) * MODW])
        m["c2"] = np.ascontiguousarray(c[[b, 1 - b], :])
        m["lams"] = lams
        m["subln"] = np.asarray(inputs["L0_attn_subln_weight"]).reshape(1, -1).astype(np.float32)
        in_maps.append(m)
    return in_maps


def kernel(**inputs):
    nc, _ = _get_program()
    in_maps = make_inmaps(inputs)
    res = run_bass_kernel_spmd(nc, in_maps, core_ids=list(range(NCORES)))
    out = np.empty((B, T, D), np.float32)
    for core in range(NCORES):
        b, tq = core // 4, core % 4
        out[b, tq * TC:(tq + 1) * TC, :] = res.results[core]["y"]
    return out
